# version 14
# baseline (speedup 1.0000x reference)
import numpy as np
import ml_dtypes
import concourse.bass as bass
import concourse.mybir as mybir
from concourse.bass_utils import run_bass_kernel_spmd
from contextlib import ExitStack

F32 = mybir.dt.float32
BF16 = mybir.dt.bfloat16
I32 = mybir.dt.int32
AX = mybir.AxisListType.X
AC = mybir.AxisListType.C
AF = mybir.ActivationFunctionType
ALU = mybir.AluOpType
BF = ml_dtypes.bfloat16
AP = bass.AP

T, C, H = 2048, 1024, 16
NT = T // 128
NCC = C // 128
STAGES = 4
DEBUG = False
NEG = -1e30
ISQ96 = 1.0 / float(np.sqrt(np.float64(96.0)))


_UID = [0]


def un(base):
    _UID[0] += 1
    return f"{base}_{_UID[0]}"


class Ctr:
    def __init__(self, sem):
        self.sem = sem
        self.n = 0

    def inc(self, k=1):
        self.n += k
        return self.n


class Sched:

    def __init__(self, nc):
        self.nc = nc
        self.q = {"tensor": [], "vector": [], "scalar": [], "gpsimd": [], "sync": []}

    def add(self, eng, fn):
        self.q[eng].append(fn)

    def run(self):
        nc = self.nc
        with nc.Block() as blk:
            if self.q["sync"]:
                @blk.sync
                def _(e, _q=self.q["sync"]):
                    for fn in _q:
                        fn(e)
            if self.q["gpsimd"]:
                @blk.gpsimd
                def _(e, _q=self.q["gpsimd"]):
                    for fn in _q:
                        fn(e)
            if self.q["tensor"]:
                @blk.tensor
                def _(e, _q=self.q["tensor"]):
                    for fn in _q:
                        fn(e)
            if self.q["vector"]:
                @blk.vector
                def _(e, _q=self.q["vector"]):
                    for fn in _q:
                        fn(e)
            if self.q["scalar"]:
                @blk.scalar
                def _(e, _q=self.q["scalar"]):
                    for fn in _q:
                        fn(e)



def _host_tables():
    t = np.arange(T, dtype=np.float64)
    inv0 = 1.0 / (10000.0 ** (np.arange(0, 32, 2, dtype=np.float64) / 32.0))
    fr0 = np.outer(t, inv0)
    emb = np.concatenate([fr0, fr0], axis=1)
    scale0 = (np.cos(emb) + np.sin(emb)).T
    ropeq0 = np.tile(scale0 * ISQ96, (4, 1)).astype(np.float32)
    ropek0 = scale0.astype(np.float32)
    inv1 = 1.0 / (10000.0 ** (np.arange(0, 64, 2, dtype=np.float64) / 64.0))
    fr1 = np.outer(t, inv1)
    cosT, sinT = np.cos(fr1).T, np.sin(fr1).T
    cq1 = np.tile(cosT * 0.125, (4, 1)).astype(np.float32)
    sq1 = np.tile(sinT * 0.125, (4, 1)).astype(np.float32)
    ck1 = np.tile(cosT, (4, 1)).astype(np.float32)
    sk1 = np.tile(sinT, (4, 1)).astype(np.float32)
    q = np.arange(128)
    maskA = np.where(q[None, :] <= q[:, None], 0.0, NEG).astype(np.float32)
    mb = np.zeros((128, 4 * 512), dtype=np.float32)
    for r in range(4):
        k = np.arange(128)[:, None]
        qq = np.arange(512)[None, :]
        mb[:, r * 512:(r + 1) * 512] = np.where(qq >= r * 128 + k, 0.0, NEG)
    ident = np.eye(128, dtype=BF)
    return dict(ropeq0=ropeq0, ropek0=ropek0, cq1=cq1, sq1=sq1, ck1=ck1, sk1=sk1,
                maskA=maskA, maskB=mb, ident=ident)


def _tile8(w):
    return w.reshape(8, 128, 1024).transpose(1, 0, 2).reshape(128, 8192)


def _host_inputs(inp):
    h = _host_tables()
    P = np.ascontiguousarray
    perm = ([hh * 64 + 2 * j for hh in range(16) for j in range(32)]
            + [hh * 64 + 2 * j + 1 for hh in range(16) for j in range(32)])
    m = {
        "wdqT": P(inp["L0_attn_W_DQ"].T),
        "wuq": P(inp["L0_attn_W_UQ"]),
        "wqr": P(inp["L0_attn_W_QR"]),
        "wdkv": P(inp["L0_attn_W_DKV"]),
        "wkr": P(inp["L0_attn_W_KR"]),
        "wuk": P(inp["L0_attn_W_UK"]),
        "wuv": P(inp["L0_attn_W_UV"]),
        "wo0": P(inp["L0_attn_W_O"]),
        "fw": P(inp["L0_ffn_weight"]),
        "gma": P(np.broadcast_to(inp["L0_ffn_ln_gamma"], (128, 1024))),
        "bta": P(np.broadcast_to(inp["L0_ffn_ln_beta"], (128, 1024))),
        "wqT1": P(inp["L1_attn_Wq"][perm, :].T),
        "wkT1": P(inp["L1_attn_Wk"][perm, :].T),
        "wvT1": P(inp["L1_attn_Wv"].T),
        "woT1": P(inp["L1_attn_Wo"].T),
        "gwT": P(inp["L1_ffn_gate_w"].T),
        "gbias": P(np.broadcast_to(inp["L1_ffn_gate_bias"], (128, 32))),
        "wgT": P(inp["L1_ffn_w_gate"].transpose(0, 2, 1)),
        "wuT": P(inp["L1_ffn_w_up"].transpose(0, 2, 1)),
        "wdT": P(inp["L1_ffn_w_down"].transpose(0, 2, 1)),
        "shgT": P(_tile8(inp["L1_ffn_sh_gate"].T)),
        "shuT": P(_tile8(inp["L1_ffn_sh_up"].T)),
        "shdT": P(_tile8(inp["L1_ffn_sh_down"].T)),
    }
    m.update(h)
    return m


class Prog:
    def __init__(self, nc, ctx):
        self.nc = nc
        e = ctx.enter_context
        dt = nc.dram_tensor
        self.x = dt("x", [T, C], F32, kind="ExternalInput")
        self.out = dt("out", [T, C], F32, kind="ExternalOutput")
        names = [("wdqT", [4096, 1024]), ("wuq", [4096, 1024]), ("wqr", [4096, 512]),
                 ("wdkv", [1024, 256]), ("wkr", [1024, 32]), ("wuk", [256, 1024]),
                 ("wuv", [256, 1024]), ("wo0", [1024, 1024]), ("fw", [1024, 1024]),
                 ("gma", [128, 1024]), ("bta", [128, 1024]),
                 ("wqT1", [1024, 1024]), ("wkT1", [1024, 1024]),
                 ("wvT1", [1024, 1024]), ("woT1", [1024, 1024]),
                 ("gwT", [1024, 32]), ("gbias", [128, 32]),
                 ("wgT", [32, 1024, 512]), ("wuT", [32, 1024, 512]),
                 ("wdT", [32, 512, 1024]),
                 ("shgT", [128, 8192]), ("shuT", [128, 8192]), ("shdT", [128, 8192]),
                 ("ropeq0", [128, T]), ("ropek0", [32, T]),
                 ("cq1", [128, T]), ("sq1", [128, T]), ("ck1", [128, T]), ("sk1", [128, T]),
                 ("maskA", [128, 128]), ("maskB", [128, 2048])]
        for n, s in names:
            setattr(self, n, dt(n, s, F32, kind="ExternalInput"))
        self.ident_in = dt("ident", [128, 128], BF16, kind="ExternalInput")
        self.dbg = {}
        if DEBUG:
            for n, sh, d in [("d_xnF", [128, 16384], BF16), ("d_qC", [128, 16384], BF16),
                             ("d_qR", [128, 16384], BF16), ("d_kR", [128, 2048], BF16),
                             ("d_cKV", [128, 4096], BF16), ("d_kpk", [128, 2048], BF16),
                             ("d_vtok", [128, 2048], BF16), ("d_OF", [128, 16384], BF16),
                             ("d_mg", [128, 2], F32), ("d_lrow", [1, 512], F32),
                             ("d_negMbc", [128, 1], F32), ("d_lbc", [128, 512], F32)]:
                self.dbg[n] = dt(n, sh, d, kind="ExternalOutput")
        self.r_a = dt("r_a", [T, C], F32)
        self.r_b = dt("r_b", [T, C], F32)
        self.cc_in = dt("cc_in", [1, 8], F32)
        self.cc_out = dt("cc_out", [1, 8], F32)
        self.wc_g = dt("wc_g", [32 * 1024, 512], BF16)
        self.wc_u = dt("wc_u", [32 * 1024, 512], BF16)
        self.wc_d = dt("wc_d", [32 * 512, 1024], BF16)
        self.wc_join = 0
        self.rep_t = 0
        self.drip = []
        self.sx = Ctr(e(nc.semaphore("sx")))
        self.sw = Ctr(e(nc.semaphore("sw")))
        self.smm = Ctr(e(nc.semaphore("smm")))
        self.sev = Ctr(e(nc.semaphore("sev")))
        self.sac = Ctr(e(nc.semaphore("sac")))
        self.sg = Ctr(e(nc.semaphore("sg")))
        self.so = Ctr(e(nc.semaphore("so")))
        self.scc = Ctr(e(nc.semaphore("scc")))
        self.sxp = [Ctr(e(nc.semaphore(f"sxp{k}"))) for k in range(2)]
        self.sop = [Ctr(e(nc.semaphore(f"sop{k}"))) for k in range(2)]
        self.swp = [Ctr(e(nc.semaphore(f"swp{k}"))) for k in range(2)]
        self.sfp = [Ctr(e(nc.semaphore(f"sfp{k}"))) for k in range(2)]
        self.sdp = [Ctr(e(nc.semaphore(f"sdp{k}"))) for k in range(2)]
        self.identity = e(nc.sbuf_tensor(un("identity"), [128, 128], BF16))
        self.ones_bf = e(nc.sbuf_tensor(un("ones_bf"), [128, 1], BF16))
        self.ones_r = e(nc.sbuf_tensor(un("ones_r"), [1, 128], F32))
        self.mAf = e(nc.sbuf_tensor(un("mAf"), [128, 128], F32))
        self.mBf = e(nc.sbuf_tensor(un("mBf"), [128, 2048], F32))
        self.idf = e(nc.sbuf_tensor(un("idfg"), [128, 128], F32))

    def take_drip(self, sch, n=1):
        for _ in range(n):
            if self.drip:
                sch.add("gpsimd", self.drip.pop(0))

    def load_consts(self):
        nc = self.nc
        t0 = self.sx.inc(48)
        with nc.Block() as blk:
            @blk.sync
            def _(s):
                s.dma_start(self.identity[:], self.ident_in.ap()).then_inc(self.sx.sem, 16)
                s.dma_start(self.mAf[:], self.maskA.ap()).then_inc(self.sx.sem, 16)
                s.dma_start(self.mBf[:], self.maskB.ap()).then_inc(self.sx.sem, 16)
                s.wait_ge(self.sx.sem, t0)

            @blk.vector
            def _(v):
                v.memset(self.ones_bf[:], 1.0)
                v.memset(self.ones_r[:], 1.0)
                v.wait_ge(self.sx.sem, t0)
                v.tensor_copy(self.idf[:], self.identity[:])


def queue_wc_cast(p):
    flat = [[65536, 16], [1, 65536]]
    chunks = []
    for dst, src in ((p.wc_g, p.wgT), (p.wc_u, p.wuT), (p.wc_d, p.wdT)):
        for i in range(16):
            chunks.append((dst, src, i * 1048576))
    p.wc_join = p.swp[0].inc(16 * len(chunks))

    def mk(dst, src, off):
        return lambda g: g.dma_start(AP(dst, off, flat),
                                     AP(src, off, flat)).then_inc(p.swp[0].sem, 16)
    p.drip = [mk(*c) for c in chunks]


def emit_taps(p, pairs):
    if not DEBUG:
        return
    nc = p.nc
    sch = Sched(nc)
    waits = (p.sev.n, p.smm.n, p.sac.n, p.sg.n)
    def f(s, waits=waits, pairs=list(pairs)):
        s.wait_ge(p.sev.sem, waits[0])
        s.wait_ge(p.smm.sem, waits[1])
        s.wait_ge(p.sac.sem, waits[2])
        s.wait_ge(p.sg.sem, waits[3])
        for name, src in pairs:
            s.dma_start(p.dbg[name].ap(), src).then_inc(p.so.sem, 16)
            p.so.inc(16)
        s.wait_ge(p.so.sem, p.so.n)
    sch.add("sync", f)
    sch.run()


def hbm_tiled_ap(t, rows, cols):
    return AP(t, 0, [[cols, 128], [128 * cols, rows // 128], [1, cols]])


def rtile(t, i):
    return AP(t, i * 128 * C, [[C, 128], [1, C]])



def emit_ln(p, ctx, r_in, xnF, keep_tok=None, per_tile_cb=None):
    nc = p.nc
    e = ctx.enter_context
    rbuf = [e(nc.sbuf_tensor(un(f"ln_r{k}"), [128, C], F32)) for k in range(2)]
    xbuf = [e(nc.sbuf_tensor(un(f"ln_x{k}"), [128, C], BF16)) for k in range(2)]
    sqs = e(nc.sbuf_tensor(un("ln_sq"), [128, C], F32))
    st = [e(nc.sbuf_tensor(un(f"ln_st{k}"), [128, 8], F32)) for k in range(2)]
    ps_tr = [e(nc.psum_tensor(un(f"ln_pt{k}"), [128, 128], BF16)) for k in range(2)]
    sch = Sched(nc)
    a3_hist = [0, 0]
    rb_sev = [0, 0]
    tr_hist = [0, 0]
    trb_free = [0, 0]
    for i in range(NT):
        b = i % 2
        s = st[b]
        xc = p.sxp[b]
        x_t = xc.inc(16)
        w_r, w_x = a3_hist[b], tr_hist[b]
        w_v = rb_sev[b]
        sch.add("sync", lambda sy, i=i, b=b, w_r=w_r, w_x=w_x, w_v=w_v: (
            sy.wait_ge(p.sac.sem, w_r),
            sy.wait_ge(p.smm.sem, w_x),
            sy.wait_ge(p.sev.sem, w_v),
            sy.dma_start(rbuf[b][:], rtile(r_in, i)).then_inc(p.sxp[b].sem, 16)))
        v1 = p.sev.inc()
        sch.add("vector", lambda v, b=b, s=s, x_t=x_t: (
            v.wait_ge(p.sxp[b].sem, x_t),
            v.reduce_sum(s[:, 0:1], rbuf[b][:], axis=AX).then_inc(p.sev.sem, 1)))
        a1 = p.sac.inc()
        sch.add("scalar", lambda a, b=b, s=s, x_t=x_t: (
            a.wait_ge(p.sxp[b].sem, x_t),
            a.activation(sqs[:], rbuf[b][:], AF.Square).then_inc(p.sac.sem, 1)))
        v2 = p.sev.inc()
        sch.add("vector", lambda v, s=s, a1=a1: (
            v.wait_ge(p.sac.sem, a1),
            v.reduce_sum(s[:, 1:2], sqs[:], axis=AX),
            v.drain(),
            v.tensor_scalar(s[:, 2:3], s[:, 0:1], 1.0 / C, None, op0=ALU.mult),
            v.drain(),
            v.tensor_tensor(s[:, 3:4], s[:, 2:3], s[:, 2:3], op=ALU.mult),
            v.tensor_scalar(s[:, 4:5], s[:, 1:2], 1.0 / C, None, op0=ALU.mult),
            v.drain(),
            v.tensor_tensor(s[:, 3:4], s[:, 4:5], s[:, 3:4], op=ALU.subtract),
            v.drain(),
            v.tensor_scalar(s[:, 3:4], s[:, 3:4], 1e-5, None,
                            op0=ALU.add).then_inc(p.sev.sem, 1)))
        a2 = p.sac.inc()
        sch.add("scalar", lambda a, s=s, v2=v2: (
            a.wait_ge(p.sev.sem, v2),
            a.activation(s[:, 4:5], s[:, 3:4], AF.Sqrt).then_inc(p.sac.sem, 1)))
        v3 = p.sev.inc()
        sch.add("vector", lambda v, s=s, a2=a2: (
            v.wait_ge(p.sac.sem, a2),
            v.reciprocal(s[:, 5:6], s[:, 4:5]),
            v.drain(),
            v.scalar_tensor_tensor(s[:, 6:7], s[:, 2:3], -1.0, s[:, 5:6],
                                   op0=ALU.mult, op1=ALU.mult).then_inc(p.sev.sem, 1)))
        a3 = p.sac.inc()
        sch.add("scalar", lambda a, b=b, s=s, v3=v3: (
            a.wait_ge(p.sev.sem, v3),
            a.activation(xbuf[b][:], rbuf[b][:], AF.Identity, bias=s[:, 6:7],
                         scale=s[:, 5:6]).then_inc(p.sac.sem, 1)))
        a3_hist[b] = a3
        if keep_tok is not None:
            v3b = p.sev.inc()
            sch.add("vector", lambda v, b=b, i=i, s=s, v3=v3: (
                v.wait_ge(p.sev.sem, v3),
                v.drain(),
                v.scalar_tensor_tensor(keep_tok[:, i * C:(i + 1) * C], rbuf[b][:],
                                       s[:, 5:6], AP(s, 6, [[8, 128], [0, C]]),
                                       op0=ALU.mult,
                                       op1=ALU.add).then_inc(p.sev.sem, 1)))
            rb_sev[b] = v3b
        last_m = 0
        for cc in range(NCC if xnF is not None else 0):
            tb = cc % 2
            m1 = p.smm.inc()
            fr = trb_free[tb]
            sch.add("tensor", lambda t, b=b, cc=cc, tb=tb, a3=a3, fr=fr: (
                t.wait_ge(p.sac.sem, a3),
                t.wait_ge(p.sev.sem, fr),
                t.transpose(ps_tr[tb][:], xbuf[b][:, cc * 128:(cc + 1) * 128],
                            p.identity[:]).then_inc(p.smm.sem, 1)))
            v4 = p.sev.inc()
            sch.add("vector", lambda v, i=i, cc=cc, tb=tb, m1=m1: (
                v.wait_ge(p.smm.sem, m1),
                v.tensor_copy(xnF[:, cc * T + i * 128: cc * T + (i + 1) * 128],
                              ps_tr[tb][:]).then_inc(p.sev.sem, 1)))
            trb_free[tb] = v4
            last_m = m1
        tr_hist[b] = last_m
        if per_tile_cb is not None:
            per_tile_cb(sch, i, rbuf[b], st[b], a3)
    sch.run()




def attn_core(p, nc, ps, ps_o, ps_l, ps_bc, PT, mg, negM, negMbc,
              lrow, linv, lbc, heads, act_prev0):
    sch = Sched(nc)
    act_prev = act_prev0
    rank1s = []
    for g in range(4):
        g_prev = p.smm.n
        nch = 0
        for qt in range(4):
            qi = g * 4 + qt
            for kc in range(0, (qi + 1) * 128, 512):
                n = min(512, (qi + 1) * 128 - kc)
                isdiag = (kc + n == (qi + 1) * 128)
                for hi, hd in enumerate(heads):
                    fa = p.sev.n
                    m1 = p.smm.inc()
                    sch.add("tensor", lambda t, hd=hd, qi=qi, kc=kc, n=n, hi=hi, fa=fa: (
                        t.wait_ge(p.sev.sem, fa),
                        hd["mmA"](t, hi, qi, kc, n, True)))
                    p.sev.inc()
                    def evf(v, hi=hi, n=n, m1=m1, mcol=hi * 17 + nch, fc=(nch == 0),
                            isdiag=isdiag):
                        v.wait_ge(p.smm.sem, m1)
                        if fc or isdiag:
                            v.drain()
                        if isdiag:
                            v.tensor_tensor(ps[hi][:, n - 128:n], ps[hi][:, n - 128:n],
                                            p.mAf[:], op=ALU.add)
                            v.drain()
                        v.reduce_max(mg[:, mcol:mcol + 1], ps[hi][:, 0:n],
                                     axis=AX).then_inc(p.sev.sem, 1)
                    sch.add("vector", evf)
                nch += 1
        for hi in range(2):
            col = hi * 4 + g
            v9 = p.sev.inc()
            sch.add("vector", lambda v, nch=nch, gp=g_prev, hi=hi: (
                v.wait_ge(p.smm.sem, gp),
                v.drain(),
                v.reduce_max(mg[:, 38 + hi:39 + hi], mg[:, hi * 17:hi * 17 + nch],
                             axis=AX).then_inc(p.sev.sem, 1)))
            m_tr = p.smm.inc()
            sch.add("tensor", lambda t, v9=v9, hi=hi: (
                t.wait_ge(p.sev.sem, v9),
                t.transpose(ps_bc[0:1, 384:512], mg[:, 38 + hi:39 + hi],
                            p.idf[:]).then_inc(p.smm.sem, 1)))
            g1 = p.sev.inc()
            sch.add("vector", lambda v, m_tr=m_tr, col=col: (
                v.wait_ge(p.smm.sem, m_tr),
                v.reduce_max(negM[0:1, col:col + 1], ps_bc[0:1, 384:512], axis=AX,
                             negate=True).then_inc(p.sev.sem, 1)))
            rank1s.append((g1, col))
    m_rk = p.smm.inc(len(rank1s))
    def rkf(t, rank1s=list(rank1s)):
        t.wait_ge(p.sev.sem, rank1s[-1][0])
        for g1, col in rank1s:
            t.matmul(ps_bc[:, col:col + 1], p.ones_r[:], negM[0:1, col:col + 1],
                     start=True, stop=True).then_inc(p.smm.sem, 1)
    sch.add("tensor", rkf)
    p.sev.inc()
    sch.add("vector", lambda v, m_rk=m_rk, ap=act_prev: (
        v.wait_ge(p.smm.sem, m_rk),
        v.wait_ge(p.sac.sem, ap),
        v.tensor_copy(negMbc[:, 0:8], ps_bc[:, 0:8]).then_inc(p.sev.sem, 1)))
    negm_ev = p.sev.n
    for g in range(4):
        po_free = p.sev.n
        nkb = 4 * (g + 1)
        for kb in range(nkb):
            r = kb - 4 * g
            deps = []
            for hi, hd in enumerate(heads):
                free_at = p.sac.n
                m1 = p.smm.inc()
                sch.add("tensor", lambda t, hd=hd, g=g, kb=kb, hi=hi, fr=free_at: (
                    t.wait_ge(p.sac.sem, fr),
                    hd["mmB"](t, hi, g, kb, True)))
                if r >= 0:
                    dv = p.sev.inc()
                    sch.add("vector", lambda v, hi=hi, r=r, m1=m1: (
                        v.wait_ge(p.smm.sem, m1),
                        v.drain(),
                        v.tensor_tensor(ps[hi][:], ps[hi][:],
                                        p.mBf[:, r * 512:(r + 1) * 512],
                                        op=ALU.add).then_inc(p.sev.sem, 1)))
                    deps.append((p.sev.sem, dv))
                else:
                    deps.append((p.smm.sem, m1))
            a1s = []
            for hi, hd in enumerate(heads):
                col = hi * 4 + g
                pb = hi * 2 + kb % 2
                pt_free = p.smm.n
                a1 = p.sac.inc()
                sch.add("scalar", lambda a, hi=hi, pb=pb, ds=deps[hi][0],
                        dv=deps[hi][1], ne=negm_ev, pf=pt_free, col=col: (
                    a.wait_ge(ds, dv),
                    a.wait_ge(p.sev.sem, ne),
                    a.wait_ge(p.smm.sem, pf),
                    a.activation(PT[pb][:], ps[hi][:], AF.Exp,
                                 bias=negMbc[:, col:col + 1]).then_inc(p.sac.sem, 1)))
                a1s.append(a1)
            for hi, hd in enumerate(heads):
                pb = hi * 2 + kb % 2
                m2 = p.smm.inc(2)
                def mmf2(t, hd=hd, hi=hi, kb=kb, pb=pb, a1=a1s[hi], nkb=nkb,
                         pf=po_free):
                    t.wait_ge(p.sac.sem, a1)
                    if kb == 0:
                        t.wait_ge(p.sev.sem, pf)
                    t.matmul(ps_l[hi][0:1, :], p.ones_bf[:], PT[pb][:],
                             start=(kb == 0), stop=(kb == nkb - 1)
                             ).then_inc(p.smm.sem, 1)
                    t.matmul(ps_o[hi][0:64, :], hd["pv_lhs"](kb), PT[pb][:],
                             start=(kb == 0), stop=(kb == nkb - 1)
                             ).then_inc(p.smm.sem, 1)
                sch.add("tensor", mmf2)
        m_last = p.smm.n
        for hi, hd in enumerate(heads):
            v1 = p.sev.inc()
            sch.add("vector", lambda v, ml=m_last, hi=hi: (
                v.wait_ge(p.smm.sem, ml),
                v.tensor_copy(lrow[:], ps_l[hi][0:1, :]),
                v.drain(),
                v.reciprocal(linv[:], lrow[:]).then_inc(p.sev.sem, 1)))
            m1 = p.smm.inc()
            sch.add("tensor", lambda t, v1=v1: (
                t.wait_ge(p.sev.sem, v1),
                t.matmul(ps_bc[:], p.ones_r[:], linv[:],
                         start=True, stop=True).then_inc(p.smm.sem, 1)))
            p.sev.inc()
            def evo(v, hd=hd, hi=hi, g=g, m1=m1):
                v.wait_ge(p.smm.sem, m1)
                v.tensor_copy(lbc[:], ps_bc[:])
                v.drain()
                v.tensor_tensor(hd["of_dst"](g), ps_o[hi][0:64, :], lbc[0:64, :],
                                op=ALU.mult).then_inc(p.sev.sem, 1)
            sch.add("vector", evo)
    p.take_drip(sch, 3)
    sch.run()


def emit_l0_attn(p, r_in, r_out):
    nc = p.nc
    with ExitStack() as ctx:
        e = ctx.enter_context
        ps = [e(nc.psum_tensor(un(f"l0ps{k}"), [128, 512], F32)) for k in range(2)]
        wqe = e(nc.sbuf_tensor(un("wqe"), [128, 8 * 1024], BF16))
        wqre = e(nc.sbuf_tensor(un("wqre"), [128, 8 * 512], BF16))
        with ExitStack() as c2:
            e2 = c2.enter_context
            wdq = e2(nc.sbuf_tensor(un("wdq"), [128, 32 * 1024], BF16))
            wuqs = e2(nc.sbuf_tensor(un("wuqs"), [128, 32 * 1024], BF16))
            wqrs = e2(nc.sbuf_tensor(un("wqrs"), [128, 32 * 512], BF16))
            pse = [e2(nc.psum_tensor(un(f"pse{k}"), [128, 512], F32)) for k in range(2)]
            sch = Sched(nc)
            w_t = p.sw.inc(48)
            sch.add("gpsimd", lambda g: (
                g.dma_start(wdq[:], hbm_tiled_ap(p.wdqT, 4096, 1024)).then_inc(p.sw.sem, 16),
                g.dma_start(wuqs[:], hbm_tiled_ap(p.wuq, 4096, 1024)).then_inc(p.sw.sem, 16),
                g.dma_start(wqrs[:], hbm_tiled_ap(p.wqr, 4096, 512)).then_inc(p.sw.sem, 16)))
            jobs = [("q", cc, oh) for cc in range(8) for oh in range(2)]
            jobs += [("r", cc, 0) for cc in range(8)]
            for bi, (kind, cc, oh) in enumerate(jobs):
                b = bi % 2
                fa = p.sev.n
                m1 = p.smm.inc()
                def mmf(t, kind=kind, cc=cc, oh=oh, b=b, w_t=w_t, fa=fa):
                    t.wait_ge(p.sw.sem, w_t)
                    t.wait_ge(p.sev.sem, fa)
                    for uc in range(32):
                        lhs = wdq[:, uc * 1024 + cc * 128: uc * 1024 + (cc + 1) * 128]
                        if kind == "q":
                            rhs = wuqs[:, uc * 1024 + oh * 512: uc * 1024 + oh * 512 + 512]
                        else:
                            rhs = wqrs[:, uc * 512:(uc + 1) * 512]
                        ins = t.matmul(pse[b][:], lhs, rhs, start=(uc == 0), stop=(uc == 31))
                    ins.then_inc(p.smm.sem, 1)
                sch.add("tensor", mmf)
                p.sev.inc()
                def evf(v, kind=kind, cc=cc, oh=oh, b=b, m1=m1):
                    v.wait_ge(p.smm.sem, m1)
                    if kind == "q":
                        dst = wqe[:, cc * 1024 + oh * 512: cc * 1024 + oh * 512 + 512]
                    else:
                        dst = wqre[:, cc * 512:(cc + 1) * 512]
                    v.tensor_copy(dst, pse[b][:]).then_inc(p.sev.sem, 1)
                sch.add("vector", evf)
            sch.run()
        cKV = e(nc.sbuf_tensor(un("cKV"), [128, 2 * T], BF16))
        kR = e(nc.sbuf_tensor(un("kR"), [128, T], BF16))
        qC = e(nc.sbuf_tensor(un("qC"), [128, 8 * T], BF16))
        qR = e(nc.sbuf_tensor(un("qR"), [128, 8 * T], BF16))
        wuks = e(nc.sbuf_tensor(un("wuks"), [128, 2 * 1024], BF16))
        wuvs = e(nc.sbuf_tensor(un("wuvs"), [128, 2 * 1024], BF16))
        kpk = e(nc.sbuf_tensor(un("kpk"), [128, T], BF16))
        vtok = e(nc.sbuf_tensor(un("vtok"), [128, 16 * 128], BF16))
        PT = [e(nc.sbuf_tensor(un(f"PT{k}"), [128, 512], BF16)) for k in range(4)]
        mg = e(nc.sbuf_tensor(un("mg"), [128, 40], F32))
        negM = e(nc.sbuf_tensor(un("negM"), [1, 8], F32))
        negMbc = e(nc.sbuf_tensor(un("negMbc"), [128, 8], F32))
        act_prev = [0]
        lrow = e(nc.sbuf_tensor(un("lrow"), [1, 512], F32))
        linv = e(nc.sbuf_tensor(un("linv"), [1, 512], F32))
        lbc = e(nc.sbuf_tensor(un("lbc"), [128, 512], F32))

        cproj = ctx.enter_context(ExitStack())
        ep = cproj.enter_context
        xnF = ep(nc.sbuf_tensor(un("xnF0"), [128, 8 * T], BF16))
        wdkvs = ep(nc.sbuf_tensor(un("wdkvs"), [128, 8 * 256], BF16))
        wkrs = ep(nc.sbuf_tensor(un("wkrs"), [128, 8 * 32], BF16))
        rq0 = ep(nc.sbuf_tensor(un("rq0"), [128, T], F32))
        rk0 = ep(nc.sbuf_tensor(un("rk0"), [32, T], F32))
        emit_ln(p, cproj, r_in, xnF)

        sch = Sched(nc)
        w_t = p.sw.inc(16 * 5)
        sch.add("gpsimd", lambda g: (
            g.dma_start(wdkvs[:], hbm_tiled_ap(p.wdkv, 1024, 256)).then_inc(p.sw.sem, 16),
            g.dma_start(wkrs[:], hbm_tiled_ap(p.wkr, 1024, 32)).then_inc(p.sw.sem, 16),
            g.dma_start(wuks[:], hbm_tiled_ap(p.wuk, 256, 1024)).then_inc(p.sw.sem, 16),
            g.dma_start(wuvs[:], hbm_tiled_ap(p.wuv, 256, 1024)).then_inc(p.sw.sem, 16),
            g.dma_start(rq0[:], p.ropeq0.ap()).then_inc(p.sw.sem, 16)))
        x_t = p.sx.inc(16)
        sch.add("sync", lambda s: (
            s.dma_start(rk0[:], p.ropek0.ap()).then_inc(p.sx.sem, 16)))
        p.sev.inc()
        sch.add("vector", lambda v: (
            v.memset(qR[32:64, :], 0.0),
            v.memset(qR[96:128, :], 0.0),
            v.memset(kR[32:64, :], 0.0),
            v.memset(kR[96:128, :], 0.0).then_inc(p.sev.sem, 1)))

        for dcc in range(2):
            for tt in range(4):
                b = (dcc * 4 + tt) % 2
                fa = p.sev.n
                m1 = p.smm.inc()
                def mmf(t, dcc=dcc, tt=tt, b=b, fa=fa, w_t=w_t):
                    t.wait_ge(p.sw.sem, w_t)
                    t.wait_ge(p.sev.sem, fa)
                    for cc in range(8):
                        ins = t.matmul(ps[b][:],
                                       wdkvs[:, cc * 256 + dcc * 128: cc * 256 + dcc * 128 + 128],
                                       xnF[:, cc * T + tt * 512: cc * T + tt * 512 + 512],
                                       start=(cc == 0), stop=(cc == 7))
                    ins.then_inc(p.smm.sem, 1)
                sch.add("tensor", mmf)
                p.sev.inc()
                sch.add("vector", lambda v, dcc=dcc, tt=tt, b=b, m1=m1: (
                    v.wait_ge(p.smm.sem, m1),
                    v.tensor_copy(cKV[:, dcc * T + tt * 512: dcc * T + tt * 512 + 512],
                                  ps[b][:]).then_inc(p.sev.sem, 1)))
        for tt in range(4):
            b = tt % 2
            fa = p.sev.n
            m1 = p.smm.inc()
            def mmf(t, tt=tt, b=b, fa=fa, x_t=x_t):
                t.wait_ge(p.sx.sem, x_t)
                t.wait_ge(p.sev.sem, fa)
                for cc in range(8):
                    ins = t.matmul(ps[b][0:32, :],
                                   wkrs[:, cc * 32:(cc + 1) * 32],
                                   xnF[:, cc * T + tt * 512: cc * T + tt * 512 + 512],
                                   start=(cc == 0), stop=(cc == 7))
                ins.then_inc(p.smm.sem, 1)
            sch.add("tensor", mmf)
            p.sev.inc()
            sch.add("vector", lambda v, tt=tt, b=b, m1=m1: (
                v.wait_ge(p.smm.sem, m1),
                v.tensor_tensor(kR[0:32, tt * 512:(tt + 1) * 512], ps[b][0:32, :],
                                rk0[0:32, tt * 512:(tt + 1) * 512],
                                op=ALU.mult).then_inc(p.sev.sem, 1)))
        kr_ev = p.sev.n
        kr_t = p.sx.inc(16)
        sch.add("sync", lambda s, kr_ev=kr_ev: (
            s.wait_ge(p.sev.sem, kr_ev),
            s.dma_start(kR[64:96, :], kR[0:32, :]).then_inc(p.sx.sem, 16)))
        for j in range(8):
            for tt in range(4):
                b = (j * 4 + tt) % 2
                fa = p.sev.n
                m1 = p.smm.inc()
                def mmf(t, j=j, tt=tt, b=b, fa=fa):
                    t.wait_ge(p.sev.sem, fa)
                    for hh in range(2):
                        h = 2 * j + hh
                        for cc in range(8):
                            ins = t.matmul(ps[b][hh * 64:hh * 64 + 64, :],
                                           wqe[:, cc * 1024 + h * 64: cc * 1024 + h * 64 + 64],
                                           xnF[:, cc * T + tt * 512: cc * T + tt * 512 + 512],
                                           start=(cc == 0), stop=(cc == 7),
                                           tile_position=(0, hh * 64))
                    ins.then_inc(p.smm.sem, 1)
                sch.add("tensor", mmf)
                p.sev.inc()
                sch.add("vector", lambda v, j=j, tt=tt, b=b, m1=m1: (
                    v.wait_ge(p.smm.sem, m1),
                    v.tensor_scalar(qC[:, j * T + tt * 512: j * T + tt * 512 + 512],
                                    ps[b][:], ISQ96, None,
                                    op0=ALU.mult).then_inc(p.sev.sem, 1)))
        for j in range(8):
            for tt in range(4):
                b = (j * 4 + tt) % 2
                fa = p.sev.n
                m1 = p.smm.inc()
                def mmf(t, j=j, tt=tt, b=b, fa=fa, w_t=w_t):
                    t.wait_ge(p.sw.sem, w_t)
                    t.wait_ge(p.sev.sem, fa)
                    for hh in range(2):
                        h = 2 * j + hh
                        for cc in range(8):
                            ins = t.matmul(ps[b][hh * 64:hh * 64 + 32, :],
                                           wqre[:, cc * 512 + h * 32: cc * 512 + h * 32 + 32],
                                           xnF[:, cc * T + tt * 512: cc * T + tt * 512 + 512],
                                           start=(cc == 0), stop=(cc == 7),
                                           tile_position=(0, hh * 64))
                    ins.then_inc(p.smm.sem, 1)
                sch.add("tensor", mmf)
                p.sev.inc()
                def evf(v, j=j, tt=tt, b=b, m1=m1):
                    v.wait_ge(p.smm.sem, m1)
                    v.tensor_tensor(qR[0:32, j * T + tt * 512: j * T + tt * 512 + 512],
                                    ps[b][0:32, :], rq0[0:32, tt * 512:(tt + 1) * 512],
                                    op=ALU.mult)
                    v.tensor_tensor(qR[64:96, j * T + tt * 512: j * T + tt * 512 + 512],
                                    ps[b][64:96, :], rq0[64:96, tt * 512:(tt + 1) * 512],
                                    op=ALU.mult).then_inc(p.sev.sem, 1)
                sch.add("vector", evf)
        kr_wait = kr_t
        sch.add("sync", lambda s: s.wait_ge(p.sx.sem, kr_wait))
        sch.run()
        emit_taps(p, [("d_xnF", xnF[:]), ("d_qC", qC[:]), ("d_qR", qR[:]),
                      ("d_kR", kR[:]), ("d_cKV", cKV[:])])
        cproj.close()

        ps_o = [e(nc.psum_tensor(un(f"l0po{k}"), [128, 512], F32)) for k in range(2)]
        ps_l = [e(nc.psum_tensor(un(f"l0pl{k}"), [128, 512], F32)) for k in range(2)]
        ps_bc = e(nc.psum_tensor(un("l0pb"), [128, 512], F32))
        OF = e(nc.sbuf_tensor(un("OF"), [128, 8 * T], BF16))
        wo0s = e(nc.sbuf_tensor(un("wo0s"), [128, 8 * 1024], BF16))
        rb2 = [e(nc.sbuf_tensor(un(f"rb2{k}"), [128, C], F32)) for k in range(2)]
        rob = [e(nc.sbuf_tensor(un(f"rob{k}"), [128, C], F32)) for k in range(2)]
        sch = Sched(nc)
        w_t = p.sw.inc(16)
        sch.add("gpsimd", lambda g: (
            g.dma_start(wo0s[:], hbm_tiled_ap(p.wo0, 1024, 1024)).then_inc(p.sw.sem, 16)))
        sch.run()

        for j in range(8):
            sch = Sched(nc)
            for tt in range(4):
                b = tt % 2
                fa = p.sev.n
                m1 = p.smm.inc()
                def mmf(t, j=j, tt=tt, b=b, fa=fa, w_t=w_t):
                    t.wait_ge(p.sw.sem, w_t)
                    t.wait_ge(p.sev.sem, fa)
                    for hh in range(2):
                        h = 2 * j + hh
                        for dcc in range(2):
                            ins = t.matmul(ps[b][hh * 64:hh * 64 + 64, :],
                                           wuks[:, dcc * 1024 + h * 64: dcc * 1024 + h * 64 + 64],
                                           cKV[:, dcc * T + tt * 512: dcc * T + tt * 512 + 512],
                                           start=(dcc == 0), stop=(dcc == 1),
                                           tile_position=(0, hh * 64))
                    ins.then_inc(p.smm.sem, 1)
                sch.add("tensor", mmf)
                p.sev.inc()
                sch.add("vector", lambda v, tt=tt, b=b, m1=m1: (
                    v.wait_ge(p.smm.sem, m1),
                    v.tensor_copy(kpk[:, tt * 512:(tt + 1) * 512],
                                  ps[b][:]).then_inc(p.sev.sem, 1)))
            for i in range(NT):
                b = i % 2
                fa = p.sev.n
                m1 = p.smm.inc()
                def mmf(t, j=j, i=i, b=b, fa=fa):
                    t.wait_ge(p.sev.sem, fa)
                    for dcc in range(2):
                        ins = t.matmul(ps[b][:, 0:128],
                                       cKV[:, dcc * T + i * 128: dcc * T + i * 128 + 128],
                                       wuvs[:, dcc * 1024 + 2 * j * 64: dcc * 1024 + 2 * j * 64 + 128],
                                       start=(dcc == 0), stop=(dcc == 1))
                    ins.then_inc(p.smm.sem, 1)
                sch.add("tensor", mmf)
                p.sev.inc()
                sch.add("vector", lambda v, i=i, b=b, m1=m1: (
                    v.wait_ge(p.smm.sem, m1),
                    v.tensor_copy(vtok[:, i * 128:(i + 1) * 128],
                                  ps[b][:, 0:128]).then_inc(p.sev.sem, 1)))
            sch.run()

            def mk_head(ro):
                def mmA(t, b, qi, kc, n, last, ro=ro, j=j):
                    t.matmul(ps[b][:, 0:n],
                             qC[ro:ro + 64, j * T + qi * 128: j * T + qi * 128 + 128],
                             kpk[ro:ro + 64, kc:kc + n],
                             start=True, stop=False, tile_position=(ro, 0))
                    ins = t.matmul(ps[b][:, 0:n],
                             qR[ro:ro + 64, j * T + qi * 128: j * T + qi * 128 + 128],
                             kR[ro:ro + 64, kc:kc + n],
                             start=False, stop=True,
                             tile_position=(ro, 0))
                    if last:
                        ins.then_inc(p.smm.sem, 1)
                def mmB(t, b, g, kb, last, ro=ro, j=j):
                    t.matmul(ps[b][:],
                             kpk[ro:ro + 64, kb * 128:(kb + 1) * 128],
                             qC[ro:ro + 64, j * T + g * 512: j * T + g * 512 + 512],
                             start=True, stop=False, tile_position=(ro, 0))
                    ins = t.matmul(ps[b][:],
                             kR[ro:ro + 64, kb * 128:(kb + 1) * 128],
                             qR[ro:ro + 64, j * T + g * 512: j * T + g * 512 + 512],
                             start=False, stop=True,
                             tile_position=(ro, 0))
                    if last:
                        ins.then_inc(p.smm.sem, 1)
                def pv_lhs(kb, ro=ro):
                    return vtok[:, kb * 128 + ro: kb * 128 + ro + 64]
                def of_dst(g, ro=ro, j=j):
                    return OF[ro:ro + 64, j * T + g * 512: j * T + g * 512 + 512]
                return dict(mmA=mmA, mmB=mmB, pv_lhs=pv_lhs, of_dst=of_dst)
            attn_core(p, nc, ps, ps_o, ps_l, ps_bc, PT, mg, negM, negMbc,
                      lrow, linv, lbc, [mk_head(0), mk_head(64)], act_prev[0])
            act_prev[0] = p.sac.n

        sch = Sched(nc)
        ot_hist = [0, 0]
        for i in range(NT):
            bb = i % 2
            x_t = p.sxp[bb].inc(16)
            ev_prev = p.sev.n
            sch.add("sync", lambda s, i=i, bb=bb, ep=ev_prev, ot=ot_hist[bb]: (
                s.wait_ge(p.sev.sem, ep),
                s.wait_ge(p.sop[bb].sem, ot),
                s.dma_start(rb2[bb][:], rtile(r_in, i)).then_inc(p.sxp[bb].sem, 16)))
            for ch in range(2):
                b = (i * 2 + ch) % 2
                fa = p.sev.n
                m1 = p.smm.inc()
                def mmf(t, i=i, ch=ch, b=b, fa=fa, w_t=w_t):
                    t.wait_ge(p.sw.sem, w_t)
                    t.wait_ge(p.sev.sem, fa)
                    for oc in range(8):
                        ins = t.matmul(ps[b][:],
                                       OF[:, oc * T + i * 128: oc * T + i * 128 + 128],
                                       wo0s[:, oc * 1024 + ch * 512: oc * 1024 + ch * 512 + 512],
                                       start=(oc == 0), stop=(oc == 7))
                    ins.then_inc(p.smm.sem, 1)
                sch.add("tensor", mmf)
                p.sev.inc()
                sch.add("vector", lambda v, ch=ch, b=b, bb=bb, m1=m1, x_t=x_t,
                        ot=ot_hist[bb]: (
                    v.wait_ge(p.smm.sem, m1),
                    v.wait_ge(p.sxp[bb].sem, x_t),
                    v.wait_ge(p.sop[bb].sem, ot),
                    v.tensor_tensor(rob[bb][:, ch * 512:(ch + 1) * 512], ps[b][:],
                                    rb2[bb][:, ch * 512:(ch + 1) * 512],
                                    op=ALU.add).then_inc(p.sev.sem, 1)))
            o_t = p.sop[bb].inc(16)
            last_v = p.sev.n
            sch.add("sync", lambda s, i=i, bb=bb, lv=last_v: (
                s.wait_ge(p.sev.sem, lv),
                s.dma_start(rtile(r_out, i), rob[bb][:]).then_inc(p.sop[bb].sem, 16)))
            ot_hist[bb] = o_t
        sch.add("sync", lambda s, n0=p.sop[0].n, n1=p.sop[1].n: (
            s.wait_ge(p.sop[0].sem, n0), s.wait_ge(p.sop[1].sem, n1)))
        sch.run()


def emit_l0_ffn(p, r_in, r_out):
    nc = p.nc
    QB, EPS, LN2E = 128.0, 1e-5, 1e-3
    with ExitStack() as ctx:
        e = ctx.enter_context
        ps = [e(nc.psum_tensor(un(f"f0ps{k}"), [128, 512], F32)) for k in range(2)]
        pst = [e(nc.psum_tensor(un(f"f0pt{k}"), [128, 128], BF16)) for k in range(2)]
        ps_bc = e(nc.psum_tensor(un("f0pb"), [128, 512], F32))
        wq = e(nc.sbuf_tensor(un("wq"), [128, 8 * 1024], BF16))
        sc = e(nc.sbuf_tensor(un("sc"), [1, 8], F32))
        bcs = e(nc.sbuf_tensor(un("bcs"), [128, 4], F32))
        gcol = e(nc.sbuf_tensor(un("gcol"), [128, 2], F32))
        stg = e(nc.sbuf_tensor(un("stg"), [1, 8], F32))

        with ExitStack() as c2:
            e2 = c2.enter_context
            fw = e2(nc.sbuf_tensor(un("fws"), [128, 8 * 1024], F32))
            fsc = e2(nc.sbuf_tensor(un("fsc"), [128, 8 * 1024], F32))
            sch = Sched(nc)
            w_t = p.sw.inc(16)
            sch.add("gpsimd", lambda g: (
                g.dma_start(fw[:], hbm_tiled_ap(p.fw, 1024, 1024)).then_inc(p.sw.sem, 16)))
            v1 = p.sev.inc()
            sch.add("vector", lambda v: (
                v.wait_ge(p.sw.sem, p.sw.n),
                v.reduce_sum(bcs[:, 3:4], fw[:], axis=AX,
                             apply_absolute_value=True).then_inc(p.sev.sem, 1)))
            mt = p.smm.inc()
            sch.add("tensor", lambda t, v1=v1: (
                t.wait_ge(p.sev.sem, v1),
                t.transpose(ps_bc[0:1, 384:512], bcs[:, 3:4],
                            p.idf[:]).then_inc(p.smm.sem, 1)))
            v2 = p.sev.inc()
            sch.add("vector", lambda v, mt=mt: (
                v.wait_ge(p.smm.sem, mt),
                v.reduce_sum(sc[:, 0:1], ps_bc[0:1, 384:512], axis=AX),
                v.drain(),
                v.tensor_scalar(sc[:, 0:1], sc[:, 0:1], 1.0 / (1024.0 * 1024.0), None,
                                op0=ALU.mult),
                v.drain(),
                v.tensor_scalar(sc[:, 5:6], sc[:, 0:1], EPS, None, op0=ALU.add),
                v.drain(),
                v.reciprocal(sc[:, 1:2], sc[:, 5:6]).then_inc(p.sev.sem, 1)))
            m1 = p.smm.inc()
            sch.add("tensor", lambda t, v2=v2: (
                t.wait_ge(p.sev.sem, v2),
                t.matmul(ps_bc[:, 0:1], p.ones_r[:], sc[:, 1:2],
                         start=True, stop=True).then_inc(p.smm.sem, 1)))
            v3 = p.sev.inc()
            sch.add("vector", lambda v, m1=m1: (
                v.wait_ge(p.smm.sem, m1),
                v.tensor_copy(bcs[:, 0:1], ps_bc[:, 0:1]),
                v.drain(),
                v.tensor_scalar(fsc[:], fw[:], bcs[:, 0:1], None,
                                op0=ALU.mult).then_inc(p.sev.sem, 1)))
            a1 = p.sac.inc()
            sch.add("scalar", lambda a, v3=v3: (
                a.wait_ge(p.sev.sem, v3),
                a.sign(fw[:], fsc[:]).then_inc(p.sac.sem, 1)))
            v4 = p.sev.inc()
            sch.add("vector", lambda v, a1=a1: (
                v.wait_ge(p.sac.sem, a1),
                v.tensor_tensor(fsc[:], fsc[:], fw[:], op=ALU.mult),
                v.drain(),
                v.tensor_scalar(fsc[:], fsc[:], 0.5, None, op0=ALU.is_gt),
                v.drain(),
                v.tensor_tensor(wq[:], fsc[:], fw[:],
                                op=ALU.mult).then_inc(p.sev.sem, 1)))
            sch.run()

        cB = ctx.enter_context(ExitStack())
        eB = cB.enter_context
        xn1 = eB(nc.sbuf_tensor(un("xn1"), [128, NT * C], F32))
        gmas = eB(nc.sbuf_tensor(un("gmas"), [128, C], F32))
        btas = eB(nc.sbuf_tensor(un("btas"), [128, C], F32))
        st2 = eB(nc.sbuf_tensor(un("st2"), [128, 8], F32))
        sq2 = eB(nc.sbuf_tensor(un("sq2"), [128, C], F32))
        tmp2 = eB(nc.sbuf_tensor(un("tmp2"), [128, C], F32))
        sch = Sched(nc)
        gb_t = p.sx.inc(32)
        sch.add("sync", lambda s: (
            s.dma_start(gmas[:], p.gma.ap()).then_inc(p.sx.sem, 16),
            s.dma_start(btas[:], p.bta.ap()).then_inc(p.sx.sem, 16)))
        sch.run()
        emit_ln(p, cB, r_in, None, keep_tok=xn1)
        sch = Sched(nc)
        for i in range(NT):
            s2 = st2
            xt = xn1[:, i * C:(i + 1) * C]
            v1 = p.sev.inc()
            sch.add("vector", lambda v, xt=xt, s2=s2: (
                v.drain(),
                v.reduce_sum(s2[:, 0:1], xt, axis=AX).then_inc(p.sev.sem, 1)))
            a1 = p.sac.inc()
            sch.add("scalar", lambda a, xt=xt, v1=v1: (
                a.wait_ge(p.sev.sem, v1),
                a.activation(sq2[:], xt, AF.Square).then_inc(p.sac.sem, 1)))
            v2 = p.sev.inc()
            sch.add("vector", lambda v, s2=s2, a1=a1: (
                v.wait_ge(p.sac.sem, a1),
                v.reduce_sum(s2[:, 1:2], sq2[:], axis=AX),
                v.drain(),
                v.tensor_scalar(s2[:, 2:3], s2[:, 0:1], 1.0 / C, None, op0=ALU.mult),
                v.drain(),
                v.tensor_tensor(s2[:, 3:4], s2[:, 2:3], s2[:, 2:3], op=ALU.mult),
                v.tensor_scalar(s2[:, 4:5], s2[:, 1:2], 1.0 / C, None, op0=ALU.mult),
                v.drain(),
                v.tensor_tensor(s2[:, 3:4], s2[:, 4:5], s2[:, 3:4], op=ALU.subtract),
                v.drain(),
                v.tensor_scalar(s2[:, 3:4], s2[:, 3:4], LN2E, None,
                                op0=ALU.add).then_inc(p.sev.sem, 1)))
            a2 = p.sac.inc()
            sch.add("scalar", lambda a, s2=s2, v2=v2: (
                a.wait_ge(p.sev.sem, v2),
                a.activation(s2[:, 4:5], s2[:, 3:4], AF.Sqrt).then_inc(p.sac.sem, 1)))
            v3 = p.sev.inc()
            sch.add("vector", lambda v, s2=s2, a2=a2: (
                v.wait_ge(p.sac.sem, a2),
                v.reciprocal(s2[:, 5:6], s2[:, 4:5]),
                v.drain(),
                v.scalar_tensor_tensor(s2[:, 6:7], s2[:, 2:3], -1.0, s2[:, 5:6],
                                       op0=ALU.mult, op1=ALU.mult).then_inc(p.sev.sem, 1)))
            a3 = p.sac.inc()
            sch.add("scalar", lambda a, xt=xt, s2=s2, v3=v3: (
                a.wait_ge(p.sev.sem, v3),
                a.wait_ge(p.sx.sem, gb_t),
                a.activation(tmp2[:], xt, AF.Identity, bias=s2[:, 6:7],
                             scale=s2[:, 5:6]).then_inc(p.sac.sem, 1)))
            v4 = p.sev.inc()
            sch.add("vector", lambda v, xt=xt, i=i, a3=a3: (
                v.wait_ge(p.sac.sem, a3),
                v.tensor_tensor(xt, tmp2[:], gmas[:], op=ALU.mult),
                v.drain(),
                v.tensor_tensor(xt, xt, btas[:], op=ALU.add),
                v.drain(),
                v.reduce_max(gcol[:, 1:2], xt, axis=AX, apply_absolute_value=True),
                v.drain(),
                (v.tensor_copy(gcol[:, 0:1], gcol[:, 1:2]) if i == 0 else
                 v.tensor_tensor(gcol[:, 0:1], gcol[:, 0:1], gcol[:, 1:2],
                                 op=ALU.max)).then_inc(p.sev.sem, 1)))
        last_ev = p.sev.n
        mt = p.smm.inc()
        sch.add("tensor", lambda t, le=last_ev: (
            t.wait_ge(p.sev.sem, le),
            t.transpose(ps_bc[0:1, 384:512], gcol[:, 0:1],
                        p.idf[:]).then_inc(p.smm.sem, 1)))
        v5 = p.sev.inc()
        sch.add("vector", lambda v, mt=mt: (
            v.memset(stg[0:1, 1:8], 0.0),
            v.wait_ge(p.smm.sem, mt),
            v.drain(),
            v.reduce_max(stg[0:1, 0:1], ps_bc[0:1, 384:512],
                         axis=AX).then_inc(p.sev.sem, 1)))
        ccd1 = p.scc.inc(16)
        ccg = p.sg.inc()
        sch.add("gpsimd", lambda g, v5=v5, ccd1=ccd1: (
            g.wait_ge(p.sev.sem, v5),
            g.dma_start(p.cc_in.ap(), stg[:]).then_inc(p.scc.sem, 16),
            g.wait_ge(p.scc.sem, ccd1),
            g.collective_compute(
                "AllReduce", ALU.max, replica_groups=[list(range(8))],
                ins=[p.cc_in.ap().opt()], outs=[p.cc_out.ap().opt()],
            ).then_inc(p.sg.sem, 1)))
        cc2 = p.scc.inc(16)
        sch.add("gpsimd", lambda g, ccg=ccg: (
            g.wait_ge(p.sg.sem, ccg),
            g.dma_start(sc[0:1, 2:3],
                        AP(p.cc_out, 0, [[8, 1], [1, 1]])).then_inc(p.scc.sem, 16)))
        v6 = p.sev.inc()
        sch.add("vector", lambda v, cc2=cc2: (
            v.wait_ge(p.scc.sem, cc2),
            v.reciprocal(sc[:, 5:6], sc[:, 2:3]),
            v.drain(),
            v.tensor_scalar(sc[:, 3:4], sc[:, 5:6], QB, None, op0=ALU.mult),
            v.tensor_tensor(sc[:, 4:5], sc[:, 2:3], sc[:, 0:1], op=ALU.mult),
            v.drain(),
            v.tensor_scalar(sc[:, 4:5], sc[:, 4:5], 1.0 / QB, None,
                            op0=ALU.mult).then_inc(p.sev.sem, 1)))
        m1 = p.smm.inc()
        sch.add("tensor", lambda t, v6=v6: (
            t.wait_ge(p.sev.sem, v6),
            t.matmul(ps_bc[:, 1:2], p.ones_r[:], sc[:, 3:4], start=True, stop=True),
            t.matmul(ps_bc[:, 2:3], p.ones_r[:], sc[:, 4:5], start=True,
                     stop=True).then_inc(p.smm.sem, 1)))
        v7 = p.sev.inc()
        sch.add("vector", lambda v, m1=m1: (
            v.wait_ge(p.smm.sem, m1),
            v.tensor_copy(bcs[:, 1:3], ps_bc[:, 1:3]).then_inc(p.sev.sem, 1)))
        sch.run()

        xqt = e(nc.sbuf_tensor(un("xqt"), [128, NT * C], BF16))
        xqF = e(nc.sbuf_tensor(un("xqF"), [128, 8 * T], BF16))
        rb2 = [e(nc.sbuf_tensor(un(f"f0rb{k}"), [128, C], F32)) for k in range(2)]
        rob = [e(nc.sbuf_tensor(un(f"f0ro{k}"), [128, C], F32)) for k in range(2)]
        sch = Sched(nc)
        v8 = p.sev.inc()
        sch.add("vector", lambda v: (
            v.drain(),
            v.tensor_scalar(xqt[:], xn1[:], bcs[:, 1:2], QB - EPS, op0=ALU.mult,
                            op1=ALU.min),
            v.drain(),
            v.tensor_scalar(xqt[:], xqt[:], -(QB - EPS), None,
                            op0=ALU.max).then_inc(p.sev.sem, 1)))
        trb_free = [0, 0]
        for i in range(NT):
            for cc in range(NCC):
                tb = (i * NCC + cc) % 2
                m1 = p.smm.inc()
                fr = trb_free[tb]
                sch.add("tensor", lambda t, i=i, cc=cc, tb=tb, fr=fr, v8=v8: (
                    t.wait_ge(p.sev.sem, v8),
                    t.wait_ge(p.sev.sem, fr),
                    t.transpose(pst[tb][:], xqt[:, i * C + cc * 128: i * C + (cc + 1) * 128],
                                p.identity[:]).then_inc(p.smm.sem, 1)))
                v9 = p.sev.inc()
                sch.add("vector", lambda v, i=i, cc=cc, tb=tb, m1=m1: (
                    v.wait_ge(p.smm.sem, m1),
                    v.tensor_copy(xqF[:, cc * T + i * 128: cc * T + (i + 1) * 128],
                                  pst[tb][:]).then_inc(p.sev.sem, 1)))
                trb_free[tb] = v9
        tr_done = p.sev.n
        ot_hist = [0, 0]
        for i in range(NT):
            bb = i % 2
            x_t = p.sxp[bb].inc(16)
            ev_prev = p.sev.n
            sch.add("sync", lambda s, i=i, bb=bb, ep=ev_prev, ot=ot_hist[bb]: (
                s.wait_ge(p.sev.sem, ep),
                s.wait_ge(p.sop[bb].sem, ot),
                s.dma_start(rb2[bb][:], rtile(r_in, i)).then_inc(p.sxp[bb].sem, 16)))
            for ch in range(2):
                b = (i * 2 + ch) % 2
                fa = p.sev.n
                m1 = p.smm.inc()
                def mmf(t, i=i, ch=ch, b=b, fa=fa, td=tr_done):
                    t.wait_ge(p.sev.sem, max(fa, td))
                    for cc in range(8):
                        ins = t.matmul(ps[b][:],
                                       xqF[:, cc * T + i * 128: cc * T + (i + 1) * 128],
                                       wq[:, cc * 1024 + ch * 512: cc * 1024 + ch * 512 + 512],
                                       start=(cc == 0), stop=(cc == 7))
                    ins.then_inc(p.smm.sem, 1)
                sch.add("tensor", mmf)
                p.sev.inc()
                sch.add("vector", lambda v, ch=ch, b=b, bb=bb, m1=m1, x_t=x_t,
                        ot=ot_hist[bb]: (
                    v.wait_ge(p.smm.sem, m1),
                    v.wait_ge(p.sxp[bb].sem, x_t),
                    v.wait_ge(p.sop[bb].sem, ot),
                    v.scalar_tensor_tensor(rob[bb][:, ch * 512:(ch + 1) * 512], ps[b][:],
                                           bcs[:, 2:3], rb2[bb][:, ch * 512:(ch + 1) * 512],
                                           op0=ALU.mult, op1=ALU.add).then_inc(p.sev.sem, 1)))
            o_t = p.sop[bb].inc(16)
            last_v = p.sev.n
            sch.add("sync", lambda s, i=i, bb=bb, lv=last_v: (
                s.wait_ge(p.sev.sem, lv),
                s.dma_start(rtile(r_out, i), rob[bb][:]).then_inc(p.sop[bb].sem, 16)))
            ot_hist[bb] = o_t
        sch.add("sync", lambda s, n0=p.sop[0].n, n1=p.sop[1].n: (
            s.wait_ge(p.sop[0].sem, n0), s.wait_ge(p.sop[1].sem, n1)))
        sch.run()


def emit_l1_attn(p, r_in, r_out):
    nc = p.nc
    with ExitStack() as ctx:
        e = ctx.enter_context
        qF = e(nc.sbuf_tensor(un("qF1"), [128, 8 * T], BF16))
        kF = e(nc.sbuf_tensor(un("kF1"), [128, 8 * T], BF16))
        vtok = e(nc.sbuf_tensor(un("vt1"), [128, NT * 1024], BF16))
        PT = [e(nc.sbuf_tensor(un(f"PT1{k}"), [128, 512], BF16)) for k in range(4)]
        mg = e(nc.sbuf_tensor(un("mg1"), [128, 40], F32))
        negM = e(nc.sbuf_tensor(un("negM1"), [1, 8], F32))
        negMbc = e(nc.sbuf_tensor(un("negMbc1"), [128, 8], F32))
        lrow = e(nc.sbuf_tensor(un("lrow1"), [1, 512], F32))
        linv = e(nc.sbuf_tensor(un("linv1"), [1, 512], F32))
        lbc = e(nc.sbuf_tensor(un("lbc1"), [128, 512], F32))

        cproj = ctx.enter_context(ExitStack())
        ep = cproj.enter_context
        ps = [ep(nc.psum_tensor(un(f"a1ps{k}"), [128, 512], F32)) for k in range(4)]
        xnF = ep(nc.sbuf_tensor(un("xnF1"), [128, 8 * T], BF16))
        wbuf = ep(nc.sbuf_tensor(un("wbuf1"), [128, 8 * 1024], BF16))
        ctab = ep(nc.sbuf_tensor(un("ctab"), [128, T], F32))
        stab = ep(nc.sbuf_tensor(un("stab"), [128, T], F32))
        rsc = [ep(nc.sbuf_tensor(un(f"rsc{k}"), [128, 512], F32)) for k in range(4)]
        emit_ln(p, cproj, r_in, xnF)

        for wsel, tab_c, tab_s, dstF in ((p.wqT1, p.cq1, p.sq1, qF),
                                         (p.wkT1, p.ck1, p.sk1, kF)):
            sch = Sched(nc)
            prev_mm = p.smm.n
            prev_ev = p.sev.n
            w_t = p.sw.inc(16)
            sch.add("gpsimd", lambda g, wsel=wsel, pm=prev_mm, w_t=w_t: (
                g.wait_ge(p.smm.sem, pm),
                g.dma_start(wbuf[:], hbm_tiled_ap(wsel, 1024, 1024)).then_inc(p.sw.sem, 16)))
            t_t = p.sx.inc(32)
            sch.add("sync", lambda s, tab_c=tab_c, tab_s=tab_s, pe=prev_ev: (
                s.wait_ge(p.sev.sem, pe),
                s.dma_start(ctab[:], tab_c.ap()).then_inc(p.sx.sem, 16),
                s.dma_start(stab[:], tab_s.ap()).then_inc(p.sx.sem, 16)))
            for tt in range(4):
                for oc in range(4):
                    b0 = (tt * 4 + oc) % 2 * 2
                    fa = p.sev.n
                    m1 = p.smm.inc()
                    def mmf(t, oc=oc, tt=tt, b0=b0, fa=fa, w_t=w_t):
                        t.wait_ge(p.sw.sem, w_t)
                        t.wait_ge(p.sev.sem, fa)
                        for half in range(2):
                            ocg = oc + 4 * half
                            for cc in range(8):
                                ins = t.matmul(
                                    ps[b0 + half][:],
                                    wbuf[:, cc * 1024 + ocg * 128: cc * 1024 + (ocg + 1) * 128],
                                    xnF[:, cc * T + tt * 512: cc * T + tt * 512 + 512],
                                    start=(cc == 0), stop=(cc == 7))
                        ins.then_inc(p.smm.sem, 1)
                    sch.add("tensor", mmf)
                    p.sev.inc()
                    def evf(v, oc=oc, tt=tt, b0=b0, m1=m1, t_t=t_t, dstF=dstF):
                        v.wait_ge(p.smm.sem, m1)
                        v.wait_ge(p.sx.sem, t_t)
                        v.drain()
                        ts = slice(tt * 512, tt * 512 + 512)
                        v.tensor_tensor(rsc[0][:], ps[b0][:], ctab[:, ts], op=ALU.mult)
                        v.tensor_tensor(rsc[1][:], ps[b0 + 1][:], stab[:, ts], op=ALU.mult)
                        v.tensor_tensor(rsc[2][:], ps[b0][:], stab[:, ts], op=ALU.mult)
                        v.tensor_tensor(rsc[3][:], ps[b0 + 1][:], ctab[:, ts], op=ALU.mult)
                        v.drain()
                        v.tensor_tensor(dstF[:, oc * T + tt * 512: oc * T + tt * 512 + 512],
                                        rsc[0][:], rsc[1][:], op=ALU.subtract)
                        v.tensor_tensor(dstF[:, (4 + oc) * T + tt * 512: (4 + oc) * T + tt * 512 + 512],
                                        rsc[2][:], rsc[3][:],
                                        op=ALU.add).then_inc(p.sev.sem, 1)
                    sch.add("vector", evf)
            sch.run()
        sch = Sched(nc)
        prev_mm = p.smm.n
        w_t = p.sw.inc(16)
        sch.add("gpsimd", lambda g, pm=prev_mm, w_t=w_t: (
            g.wait_ge(p.smm.sem, pm),
            g.dma_start(wbuf[:], hbm_tiled_ap(p.wvT1, 1024, 1024)).then_inc(p.sw.sem, 16)))
        for i in range(NT):
            for oh in range(2):
                b = (i * 2 + oh) % 2
                fa = p.sev.n
                m1 = p.smm.inc()
                def mmf(t, i=i, oh=oh, b=b, fa=fa, w_t=w_t):
                    t.wait_ge(p.sw.sem, w_t)
                    t.wait_ge(p.sev.sem, fa)
                    for cc in range(8):
                        ins = t.matmul(ps[b][:],
                                       xnF[:, cc * T + i * 128: cc * T + (i + 1) * 128],
                                       wbuf[:, cc * 1024 + oh * 512: cc * 1024 + oh * 512 + 512],
                                       start=(cc == 0), stop=(cc == 7))
                    ins.then_inc(p.smm.sem, 1)
                sch.add("tensor", mmf)
                p.sev.inc()
                sch.add("vector", lambda v, i=i, oh=oh, b=b, m1=m1: (
                    v.wait_ge(p.smm.sem, m1),
                    v.tensor_copy(vtok[:, i * 1024 + oh * 512: i * 1024 + oh * 512 + 512],
                                  ps[b][:]).then_inc(p.sev.sem, 1)))
        sch.run()
        cproj.close()

        ps = [e(nc.psum_tensor(un(f"a1pt{k}"), [128, 512], F32)) for k in range(2)]
        ps_o = [e(nc.psum_tensor(un(f"a1po{k}"), [128, 512], F32)) for k in range(2)]
        ps_l = [e(nc.psum_tensor(un(f"a1pl{k}"), [128, 512], F32)) for k in range(2)]
        ps_bc = e(nc.psum_tensor(un("a1pb"), [128, 512], F32))
        OF = e(nc.sbuf_tensor(un("OF1"), [128, 8 * T], BF16))
        wos = e(nc.sbuf_tensor(un("wos1"), [128, 8 * 1024], BF16))
        rb2 = [e(nc.sbuf_tensor(un(f"a1rb{k}"), [128, C], F32)) for k in range(2)]
        rob = [e(nc.sbuf_tensor(un(f"a1ro{k}"), [128, C], F32)) for k in range(2)]
        sch = Sched(nc)
        w_t = p.sw.inc(16)
        sch.add("gpsimd", lambda g, w_t=w_t: (
            g.dma_start(wos[:], hbm_tiled_ap(p.woT1, 1024, 1024)).then_inc(p.sw.sem, 16)))
        sch.run()

        act_prev = [0]
        for hp in range(8):
            def mk_head(h):
                cc = h // 4
                ro = 32 * (h % 4)
                ro2 = 64 * (h % 2)
                jc = h // 2
                def mmA(t, b, qi, kc, n, last, cc=cc, ro=ro):
                    t.matmul(ps[b][:, 0:n],
                             qF[ro:ro + 32, cc * T + qi * 128: cc * T + qi * 128 + 128],
                             kF[ro:ro + 32, cc * T + kc: cc * T + kc + n],
                             start=True, stop=False, tile_position=(ro, 0))
                    ins = t.matmul(ps[b][:, 0:n],
                             qF[ro:ro + 32, (4 + cc) * T + qi * 128: (4 + cc) * T + qi * 128 + 128],
                             kF[ro:ro + 32, (4 + cc) * T + kc: (4 + cc) * T + kc + n],
                             start=False, stop=True,
                             tile_position=(ro, 0))
                    if last:
                        ins.then_inc(p.smm.sem, 1)
                def mmB(t, b, g, kb, last, cc=cc, ro=ro):
                    t.matmul(ps[b][:],
                             kF[ro:ro + 32, cc * T + kb * 128: cc * T + (kb + 1) * 128],
                             qF[ro:ro + 32, cc * T + g * 512: cc * T + g * 512 + 512],
                             start=True, stop=False, tile_position=(ro, 0))
                    ins = t.matmul(ps[b][:],
                             kF[ro:ro + 32, (4 + cc) * T + kb * 128: (4 + cc) * T + (kb + 1) * 128],
                             qF[ro:ro + 32, (4 + cc) * T + g * 512: (4 + cc) * T + g * 512 + 512],
                             start=False, stop=True,
                             tile_position=(ro, 0))
                    if last:
                        ins.then_inc(p.smm.sem, 1)
                def pv_lhs(kb, h=h):
                    return vtok[:, kb * 1024 + h * 64: kb * 1024 + h * 64 + 64]
                def of_dst(g, ro2=ro2, jc=jc):
                    return OF[ro2:ro2 + 64, jc * T + g * 512: jc * T + g * 512 + 512]
                return dict(mmA=mmA, mmB=mmB, pv_lhs=pv_lhs, of_dst=of_dst)
            attn_core(p, nc, ps, ps_o, ps_l, ps_bc, PT, mg, negM, negMbc,
                      lrow, linv, lbc, [mk_head(2 * hp), mk_head(2 * hp + 1)],
                      act_prev[0])
            act_prev[0] = p.sac.n

        sch = Sched(nc)
        ot_hist = [0, 0]
        for i in range(NT):
            bb = i % 2
            x_t = p.sxp[bb].inc(16)
            ev_prev = p.sev.n
            sch.add("sync", lambda s, i=i, bb=bb, ep=ev_prev, ot=ot_hist[bb]: (
                s.wait_ge(p.sev.sem, ep),
                s.wait_ge(p.sop[bb].sem, ot),
                s.dma_start(rb2[bb][:], rtile(r_in, i)).then_inc(p.sxp[bb].sem, 16)))
            for ch in range(2):
                b = (i * 2 + ch) % 2
                fa = p.sev.n
                m1 = p.smm.inc()
                def mmf(t, i=i, ch=ch, b=b, fa=fa, w_t=w_t):
                    t.wait_ge(p.sw.sem, w_t)
                    t.wait_ge(p.sev.sem, fa)
                    for oc in range(8):
                        ins = t.matmul(ps[b][:],
                                       OF[:, oc * T + i * 128: oc * T + i * 128 + 128],
                                       wos[:, oc * 1024 + ch * 512: oc * 1024 + ch * 512 + 512],
                                       start=(oc == 0), stop=(oc == 7))
                    ins.then_inc(p.smm.sem, 1)
                sch.add("tensor", mmf)
                p.sev.inc()
                sch.add("vector", lambda v, ch=ch, b=b, bb=bb, m1=m1, x_t=x_t,
                        ot=ot_hist[bb]: (
                    v.wait_ge(p.smm.sem, m1),
                    v.wait_ge(p.sxp[bb].sem, x_t),
                    v.wait_ge(p.sop[bb].sem, ot),
                    v.tensor_tensor(rob[bb][:, ch * 512:(ch + 1) * 512], ps[b][:],
                                    rb2[bb][:, ch * 512:(ch + 1) * 512],
                                    op=ALU.add).then_inc(p.sev.sem, 1)))
            o_t = p.sop[bb].inc(16)
            last_v = p.sev.n
            sch.add("sync", lambda s, i=i, bb=bb, lv=last_v: (
                s.wait_ge(p.sev.sem, lv),
                s.dma_start(rtile(r_out, i), rob[bb][:]).then_inc(p.sop[bb].sem, 16)))
            ot_hist[bb] = o_t
        sch.add("sync", lambda s, n0=p.sop[0].n, n1=p.sop[1].n: (
            s.wait_ge(p.sop[0].sem, n0), s.wait_ge(p.sop[1].sem, n1)))
        sch.run()


def emit_l1_ffn(p, r_in, r_out):
    nc = p.nc
    BIG = 1e30
    with ExitStack() as ctx:
        e = ctx.enter_context
        xnF = e(nc.sbuf_tensor(un("xnF2"), [128, 8 * T], BF16))
        lgt = e(nc.sbuf_tensor(un("lgt"), [128, NT * 32], F32))
        gbs = e(nc.sbuf_tensor(un("gbs"), [128, 32], F32))
        cmb = e(nc.sbuf_tensor(un("cmb"), [128, NT * 32], F32))
        cmbT4 = e(nc.sbuf_tensor(un("cmbT4"), [128, T], F32))
        idf = e(nc.sbuf_tensor(un("idf"), [128, 128], F32))

        sch = Sched(nc)
        p.sev.inc()
        sch.add("vector", lambda v: (
            v.tensor_copy(idf[:], p.identity[:]).then_inc(p.sev.sem, 1)))
        p.take_drip(sch, len(p.drip))
        sch.add("sync", lambda s, wj=p.wc_join: s.wait_ge(p.swp[0].sem, wj))
        sch.run()

        with ExitStack() as cB:
            eB = cB.enter_context
            xn1 = eB(nc.sbuf_tensor(un("xn1g"), [128, NT * C], F32))
            gws = eB(nc.sbuf_tensor(un("gws"), [128, 8 * 32], F32))
            scrt = eB(nc.sbuf_tensor(un("scrt"), [128, 32], F32))
            scb = eB(nc.sbuf_tensor(un("scb"), [128, 32], F32))
            grp = eB(nc.sbuf_tensor(un("grp"), [128, 16], F32))
            msk = eB(nc.sbuf_tensor(un("msk"), [128, 32], F32))
            mw = eB(nc.sbuf_tensor(un("mw"), [128, 32], F32))
            xT32 = eB(nc.sbuf_tensor(un("xT32"), [128, C], F32))
            pst = [eB(nc.psum_tensor(un(f"gpt{k}"), [128, 128], F32)) for k in range(2)]
            ps_g = eB(nc.psum_tensor(un("psg"), [128, 512], F32))
            scrt = eB(nc.sbuf_tensor(un("scrt"), [128, 32], F32))
            scb = eB(nc.sbuf_tensor(un("scb"), [128, 32], F32))
            grp = eB(nc.sbuf_tensor(un("grp"), [128, 16], F32))
            msk = eB(nc.sbuf_tensor(un("msk"), [128, 32], F32))
            mw = eB(nc.sbuf_tensor(un("mw"), [128, 32], F32))
            sch = Sched(nc)
            g_t = p.sx.inc(32)
            sch.add("sync", lambda s: (
                s.dma_start(gws[:], hbm_tiled_ap(p.gwT, 1024, 32)).then_inc(p.sx.sem, 16),
                s.dma_start(gbs[:], p.gbias.ap()).then_inc(p.sx.sem, 16)))
            sch.run()
            emit_ln(p, cB, r_in, xnF, keep_tok=xn1)
            def mk_route(i, a1):
                def route(v, i=i, a1=a1):
                        v.wait_ge(p.sac.sem, a1)
                        v.drain()
                        v.tensor_tensor(scb[:], scrt[:], gbs[:], op=ALU.add)
                        v.drain()
                        def g4(off):
                            return AP(scb, off, [[32, 128], [4, 8]])
                        v.tensor_tensor(grp[:, 0:1].tensor[:, 0:8] if False else
                                        AP(grp, 0, [[16, 128], [1, 8]]),
                                        g4(0), g4(1), op=ALU.max)
                        v.tensor_tensor(AP(grp, 8 * 0 + 0, [[16, 128], [1, 8]]),
                                        g4(0), g4(1), op=ALU.max) if False else None
                        v.drain()
                        v.tensor_tensor(AP(msk, 0, [[32, 128], [1, 8]]), g4(0), g4(1),
                                        op=ALU.min)
                        v.tensor_tensor(AP(msk, 8, [[32, 128], [1, 8]]), g4(2), g4(3),
                                        op=ALU.min)
                        v.tensor_tensor(AP(msk, 16, [[32, 128], [1, 8]]), g4(2), g4(3),
                                        op=ALU.max)
                        v.drain()
                        v.tensor_tensor(AP(grp, 8, [[16, 128], [1, 8]]),
                                        AP(grp, 0, [[16, 128], [1, 8]]),
                                        AP(msk, 16, [[32, 128], [1, 8]]), op=ALU.max)
                        v.tensor_tensor(AP(mw, 0, [[32, 128], [1, 8]]),
                                        AP(grp, 0, [[16, 128], [1, 8]]),
                                        AP(msk, 16, [[32, 128], [1, 8]]), op=ALU.min)
                        v.tensor_tensor(AP(mw, 8, [[32, 128], [1, 8]]),
                                        AP(msk, 0, [[32, 128], [1, 8]]),
                                        AP(msk, 8, [[32, 128], [1, 8]]), op=ALU.max)
                        v.drain()
                        v.tensor_tensor(AP(mw, 16, [[32, 128], [1, 8]]),
                                        AP(mw, 0, [[32, 128], [1, 8]]),
                                        AP(mw, 8, [[32, 128], [1, 8]]), op=ALU.max)
                        v.drain()
                        v.tensor_tensor(AP(grp, 0, [[16, 128], [1, 8]]),
                                        AP(grp, 8, [[16, 128], [1, 8]]),
                                        AP(mw, 16, [[32, 128], [1, 8]]), op=ALU.add)
                        v.drain()
                        v.tensor_copy(AP(mw, 24, [[32, 128], [1, 8]]),
                                      AP(grp, 0, [[16, 128], [1, 8]]))
                        for _ in range(3):
                            v.drain()
                            v.reduce_max(grp[:, 8:9], AP(mw, 24, [[32, 128], [1, 8]]),
                                         axis=AX)
                            v.drain()
                            v.tensor_scalar(AP(msk, 0, [[32, 128], [1, 8]]),
                                            AP(mw, 24, [[32, 128], [1, 8]]),
                                            grp[:, 8:9], None, op0=ALU.is_ge)
                            v.drain()
                            v.scalar_tensor_tensor(AP(mw, 24, [[32, 128], [1, 8]]),
                                                   AP(msk, 0, [[32, 128], [1, 8]]),
                                                   -BIG, AP(mw, 24, [[32, 128], [1, 8]]),
                                                   op0=ALU.mult, op1=ALU.add)
                        v.drain()
                        v.reduce_max(grp[:, 8:9], AP(mw, 24, [[32, 128], [1, 8]]), axis=AX)
                        v.drain()
                        v.tensor_scalar(AP(msk, 0, [[32, 128], [1, 8]]),
                                        AP(grp, 0, [[16, 128], [1, 8]]),
                                        grp[:, 8:9], None, op0=ALU.is_ge)
                        v.drain()
                        v.tensor_copy(AP(mw, 0, [[32, 128], [4, 8], [1, 4]]),
                                      AP(msk, 0, [[32, 128], [1, 8], [0, 4]]))
                        v.drain()
                        v.tensor_tensor(msk[:], scb[:], mw[:], op=ALU.mult)
                        v.drain()
                        v.tensor_copy(mw[:], msk[:])
                        for _ in range(7):
                            v.drain()
                            v.reduce_max(grp[:, 8:9], mw[:], axis=AX)
                            v.drain()
                            v.tensor_scalar(scb[:], mw[:], grp[:, 8:9], None, op0=ALU.is_ge)
                            v.drain()
                            v.scalar_tensor_tensor(mw[:], scb[:], -BIG, mw[:],
                                                   op0=ALU.mult, op1=ALU.add)
                        v.drain()
                        v.reduce_max(grp[:, 8:9], mw[:], axis=AX)
                        v.drain()
                        v.tensor_scalar(scb[:], msk[:], grp[:, 8:9], None, op0=ALU.is_ge)
                        v.drain()
                        v.tensor_tensor(msk[:], scrt[:], scb[:], op=ALU.mult)
                        v.drain()
                        v.reduce_sum(grp[:, 9:10], msk[:], axis=AX)
                        v.drain()
                        v.tensor_scalar(grp[:, 9:10], grp[:, 9:10], 1e-20, None, op0=ALU.add)
                        v.drain()
                        v.reciprocal(grp[:, 10:11], grp[:, 9:10])
                        v.drain()
                        v.tensor_scalar(cmb[:, i * 32:(i + 1) * 32], msk[:],
                                        grp[:, 10:11], 2.5, op0=ALU.mult,
                                        op1=ALU.mult).then_inc(p.sev.sem, 1)
                    
                return route
            sch = Sched(nc)
            for i in range(NT):
                for cc in range(NCC):
                    tb = cc % 2
                    m1 = p.smm.inc()
                    fr = p.sev.n
                    sch.add("tensor", lambda t, i=i, cc=cc, tb=tb, fr=fr: (
                        t.wait_ge(p.sev.sem, fr),
                        t.transpose(pst[tb][:], xn1[:, i * C + cc * 128: i * C + (cc + 1) * 128],
                                    idf[:]).then_inc(p.smm.sem, 1)))
                    p.sev.inc()
                    sch.add("vector", lambda v, cc=cc, tb=tb, m1=m1: (
                        v.wait_ge(p.smm.sem, m1),
                        v.tensor_copy(xT32[:, cc * 128:(cc + 1) * 128],
                                      pst[tb][:]).then_inc(p.sev.sem, 1)))
                tr_ev = p.sev.n
                m2 = p.smm.inc()
                sch.add("tensor", lambda t, i=i, te=tr_ev, g_t=g_t: (
                    t.wait_ge(p.sev.sem, te),
                    t.wait_ge(p.sx.sem, g_t),
                    [t.matmul(ps_g[:, 0:32], xT32[:, cc * 128:(cc + 1) * 128],
                              gws[:, cc * 32:(cc + 1) * 32],
                              start=(cc == 0), stop=(cc == 7)) for cc in range(8)][-1]
                    .then_inc(p.smm.sem, 1)))
                vg = p.sev.inc()
                sch.add("vector", lambda v, i=i, m2=m2: (
                    v.wait_ge(p.smm.sem, m2),
                    v.tensor_copy(lgt[:, i * 32:(i + 1) * 32],
                                  ps_g[:, 0:32]).then_inc(p.sev.sem, 1)))
                a1 = p.sac.inc()
                sch.add("scalar", lambda a, i=i, vg=vg: (
                    a.wait_ge(p.sev.sem, vg),
                    a.activation(scrt[:], lgt[:, i * 32:(i + 1) * 32],
                                 AF.Sigmoid).then_inc(p.sac.sem, 1)))
                p.sev.inc()
                sch.add("vector", mk_route(i, a1))
                rt_ev = p.sev.n
                m3 = p.smm.inc()
                sch.add("tensor", lambda t, i=i, re=rt_ev: (
                    t.wait_ge(p.sev.sem, re),
                    t.transpose(pst[0][0:32, 0:128], cmb[:, i * 32:(i + 1) * 32],
                                p.idf[:]).then_inc(p.smm.sem, 1)))
                p.sev.inc()
                sch.add("vector", lambda v, i=i, m3=m3: (
                    v.wait_ge(p.smm.sem, m3),
                    v.tensor_copy(cmbT4[0:32, i * 128:(i + 1) * 128],
                                  pst[0][0:32, 0:128]).then_inc(p.sev.sem, 1)))
            rep_ev = p.sev.n
            p.rep_t = p.sx.inc(48)
            sch.add("sync", lambda s, re=rep_ev, rt=p.rep_t: (
                s.wait_ge(p.sev.sem, re),
                s.dma_start(cmbT4[32:64, :], cmbT4[0:32, :]).then_inc(p.sx.sem, 16),
                s.dma_start(cmbT4[64:96, :], cmbT4[0:32, :]).then_inc(p.sx.sem, 16),
                s.dma_start(cmbT4[96:128, :], cmbT4[0:32, :]).then_inc(p.sx.sem, 16)))
            sch.run()
        with ExitStack() as cC:
            eC = cC.enter_context
            wg = [eC(nc.sbuf_tensor(un(f"wgb{k}"), [128, 8 * 512], BF16)) for k in range(2)]
            wu = [eC(nc.sbuf_tensor(un(f"wub{k}"), [128, 8 * 512], BF16)) for k in range(2)]
            wd = [eC(nc.sbuf_tensor(un(f"wdb{k}"), [128, 4 * 1024], BF16)) for k in range(2)]
            Gs = [eC(nc.sbuf_tensor(un(f"Gs{k}"), [128, 4 * 256], BF16)) for k in range(2)]
            Us = [eC(nc.sbuf_tensor(un(f"Us{k}"), [128, 4 * 256], BF16)) for k in range(2)]
            Ps = [eC(nc.sbuf_tensor(un(f"Ps{k}"), [128, 4 * 256], BF16)) for k in range(2)]
            cbc = [eC(nc.sbuf_tensor(un(f"cbc{k}"), [128, 256], F32)) for k in range(2)]
            routed = eC(nc.sbuf_tensor(un("routed"), [128, NT * C], F32))
            psb = [eC(nc.psum_tensor(un(f"psb{k}"), [128, 512], F32)) for k in range(4)]
            dn = [eC(nc.psum_tensor(un(f"dn{k}"), [128, 512], F32)) for k in range(4)]
            last_use = {}
            pend = [None]
            rt_in = p.sw.inc(16 * NT)

            def preres(g):
                for i in range(NT):
                    g.dma_start(routed[:, i * C:(i + 1) * C],
                                rtile(r_in, i)).then_inc(p.sw.sem, 16)

            def emit_down(sch, pd):
                (ee, tcb, par, v1) = pd
                tp = tcb % 2
                pdpar = par
                for q in range(4):
                    tsub, hh = q // 2, q % 2
                    bq = q
                    dfree = p.sev.n
                    m2 = p.smm.inc()
                    def dmf(t, par=par, tsub=tsub, hh=hh, bq=bq, v1=v1, df=dfree,
                            tp=tp):
                        t.wait_ge(p.sev.sem, max(v1, df))
                        for ic in range(4):
                            ins = t.matmul(
                                dn[bq][:],
                                Ps[tp][:, ic * 256 + tsub * 128: ic * 256 + tsub * 128 + 128],
                                wd[par][:, ic * 1024 + hh * 512: ic * 1024 + hh * 512 + 512],
                                start=(ic == 0), stop=(ic == 3))
                        ins.then_inc(p.smm.sem, 1)
                    sch.add("tensor", dmf)
                    p.sev.inc()
                    def evd(v, ee=ee, tcb=tcb, tsub=tsub, hh=hh, bq=bq, m2=m2):
                        v.wait_ge(p.smm.sem, m2)
                        if ee == 0:
                            v.wait_ge(p.sw.sem, rt_in)
                        dst = routed[:, (tcb * 2 + tsub) * C + hh * 512:
                                     (tcb * 2 + tsub) * C + hh * 512 + 512]
                        v.tensor_tensor(dst, dst, dn[bq][:],
                                        op=ALU.add).then_inc(p.sev.sem, 1)
                    sch.add("vector", evd)
                last_use[pdpar] = p.smm.n

            def emit_store(sch, tcb_done):
                fin = p.sev.n
                def stf(s, fin=fin, tcb_done=tcb_done):
                    s.wait_ge(p.sev.sem, fin)
                    for tsub in range(2):
                        i = tcb_done * 2 + tsub
                        s.dma_start(rtile(p.r_b, i), routed[:, i * C:(i + 1) * C]
                                    ).then_inc(p.sop[i % 2].sem, 16)
                sch.add("sync", stf)
                for tsub in range(2):
                    p.sop[(tcb_done * 2 + tsub) % 2].inc(16)

            for ee in range(32):
                par = ee % 2
                sch = Sched(nc)
                if ee == 0:
                    sch.add("gpsimd", preres)
                pu = last_use.get(par, 0)
                w_t = p.sfp[par].inc(48)
                def ldf(s, ee=ee, par=par, pu=pu):
                    s.wait_ge(p.smm.sem, pu)
                    base = ee * 512 * 1024
                    s.dma_start(wg[par][:], AP(p.wc_g, base,
                                               [[512, 128], [128 * 512, 8], [1, 512]])
                                ).then_inc(p.sfp[par].sem, 16)
                    s.dma_start(wu[par][:], AP(p.wc_u, base,
                                               [[512, 128], [128 * 512, 8], [1, 512]])
                                ).then_inc(p.sfp[par].sem, 16)
                    s.dma_start(wd[par][:], AP(p.wc_d, base,
                                               [[1024, 128], [128 * 1024, 4], [1, 1024]])
                                ).then_inc(p.sfp[par].sem, 16)
                sch.add("sync", ldf)
                for tcb in range(8):
                    pi = ee * 8 + tcb
                    gp_ = (pi % 2) * 2
                    up_ = ((pi + 1) % 2) * 2
                    gb = (gp_, gp_ + 1)
                    ub = (up_, up_ + 1)
                    tp = tcb % 2
                    gu_free = (p.sac.n, p.sev.n)
                    v0 = p.sev.inc()
                    sch.add("vector", lambda v, ee=ee, tcb=tcb, tp=tp, rt=p.rep_t: (
                        v.wait_ge(p.sx.sem, rt),
                        v.drain(),
                        v.stream_shuffle(cbc[tp][:], cmbT4[:, tcb * 256:(tcb + 1) * 256],
                                         [ee] * 32).then_inc(p.sev.sem, 1)))
                    m1 = p.smm.inc()
                    def gmf(t, tcb=tcb, par=par, w_t=w_t, gf=gu_free, gb=gb, ub=ub):
                        t.wait_ge(p.sfp[par].sem, w_t)
                        t.wait_ge(p.sac.sem, gf[0])
                        t.wait_ge(p.sev.sem, gf[1])
                        for ic in range(4):
                            bank = gb[ic // 2]
                            col = (ic % 2) * 256
                            for cc in range(8):
                                ins = t.matmul(
                                    psb[bank][:, col:col + 256],
                                    wg[par][:, cc * 512 + ic * 128: cc * 512 + (ic + 1) * 128],
                                    xnF[:, cc * T + tcb * 256: cc * T + (tcb + 1) * 256],
                                    start=(cc == 0), stop=(cc == 7))
                        for ic in range(4):
                            bank = ub[ic // 2]
                            col = (ic % 2) * 256
                            for cc in range(8):
                                ins = t.matmul(
                                    psb[bank][:, col:col + 256],
                                    wu[par][:, cc * 512 + ic * 128: cc * 512 + (ic + 1) * 128],
                                    xnF[:, cc * T + tcb * 256: cc * T + (tcb + 1) * 256],
                                    start=(cc == 0), stop=(cc == 7))
                        ins.then_inc(p.smm.sem, 1)
                    sch.add("tensor", gmf)
                    if pend[0] is not None:
                        emit_down(sch, pend[0])
                        if pend[0][0] == 31:
                            emit_store(sch, pend[0][1])
                    a1 = p.sac.inc()
                    def sif(a, m1=m1, gb=gb, tp=tp):
                        a.wait_ge(p.smm.sem, m1)
                        for ic in range(4):
                            bank = gb[ic // 2]
                            col = (ic % 2) * 256
                            ins = a.activation(Gs[tp][:, ic * 256:(ic + 1) * 256],
                                               psb[bank][:, col:col + 256], AF.Silu)
                        ins.then_inc(p.sac.sem, 1)
                    sch.add("scalar", sif)
                    v1 = p.sev.inc()
                    def upf(v, m1=m1, a1=a1, v0=v0, ub=ub, tp=tp):
                        v.wait_ge(p.smm.sem, m1)
                        v.wait_ge(p.sac.sem, a1)
                        v.wait_ge(p.sev.sem, v0)
                        v.drain()
                        for ic in range(4):
                            bank = ub[ic // 2]
                            col = (ic % 2) * 256
                            v.tensor_tensor(Us[tp][:, ic * 256:(ic + 1) * 256],
                                            psb[bank][:, col:col + 256], cbc[tp][:],
                                            op=ALU.mult)
                        v.drain()
                        for ic in range(4):
                            ins = v.tensor_tensor(Ps[tp][:, ic * 256:(ic + 1) * 256],
                                                  Gs[tp][:, ic * 256:(ic + 1) * 256],
                                                  Us[tp][:, ic * 256:(ic + 1) * 256],
                                                  op=ALU.mult)
                        ins.then_inc(p.sev.sem, 1)
                    sch.add("vector", upf)
                    pend[0] = (ee, tcb, par, v1)
                    last_use[par] = p.smm.n
                sch.run()
            sch = Sched(nc)
            emit_down(sch, pend[0])
            emit_store(sch, 7)
            sch.add("sync", lambda s, n0=p.sop[0].n, n1=p.sop[1].n: (
                s.wait_ge(p.sop[0].sem, n0), s.wait_ge(p.sop[1].sem, n1)))
            sch.run()

        with ExitStack() as cD:
            eD = cD.enter_context
            shg = eD(nc.sbuf_tensor(un("shg"), [128, 8 * 1024], BF16))
            shu = eD(nc.sbuf_tensor(un("shu"), [128, 8 * 1024], BF16))
            shd = eD(nc.sbuf_tensor(un("shd"), [128, 8 * 1024], BF16))
            PsS = eD(nc.sbuf_tensor(un("PsS"), [128, 8 * T], BF16))
            gsc = eD(nc.sbuf_tensor(un("gsc2"), [128, 512], BF16))
            rb2 = [eD(nc.sbuf_tensor(un(f"shrb{k}"), [128, C], F32)) for k in range(2)]
            rob = [eD(nc.sbuf_tensor(un(f"shro{k}"), [128, C], F32)) for k in range(2)]
            psA = [eD(nc.psum_tensor(un(f"psA{k}"), [128, 512], F32)) for k in range(2)]
            psB = [eD(nc.psum_tensor(un(f"psB{k}"), [128, 512], F32)) for k in range(2)]
            sch = Sched(nc)
            w_t0 = p.sw.inc(32)
            w_t = p.scc.inc(16)
            tflat = [[8192, 128], [1, 8192]]
            sch.add("gpsimd", lambda g: (
                g.dma_start(shg[:], AP(p.shgT, 0, tflat)).then_inc(p.sw.sem, 16),
                g.dma_start(shu[:], AP(p.shuT, 0, tflat)).then_inc(p.sw.sem, 16),
                g.dma_start(shd[:], AP(p.shdT, 0, tflat)).then_inc(p.scc.sem, 16)))
            for i2c in range(8):
                for tt in range(4):
                    b = (i2c * 4 + tt) % 2
                    fa = p.sev.n
                    m1 = p.smm.inc()
                    def gmf(t, i2c=i2c, tt=tt, b=b, fa=fa, w_t=w_t0):
                        t.wait_ge(p.sw.sem, w_t)
                        t.wait_ge(p.sev.sem, fa)
                        for cc in range(8):
                            t.matmul(psA[b][:],
                                     shg[:, cc * 1024 + i2c * 128: cc * 1024 + (i2c + 1) * 128],
                                     xnF[:, cc * T + tt * 512: cc * T + tt * 512 + 512],
                                     start=(cc == 0), stop=(cc == 7))
                        for cc in range(8):
                            ins = t.matmul(psB[b][:],
                                     shu[:, cc * 1024 + i2c * 128: cc * 1024 + (i2c + 1) * 128],
                                     xnF[:, cc * T + tt * 512: cc * T + tt * 512 + 512],
                                     start=(cc == 0), stop=(cc == 7))
                        ins.then_inc(p.smm.sem, 1)
                    sch.add("tensor", gmf)
                    a1 = p.sac.inc()
                    sch.add("scalar", lambda a, b=b, m1=m1: (
                        a.wait_ge(p.smm.sem, m1),
                        a.activation(gsc[:], psA[b][:], AF.Silu).then_inc(p.sac.sem, 1)))
                    p.sev.inc()
                    sch.add("vector", lambda v, i2c=i2c, tt=tt, b=b, a1=a1: (
                        v.wait_ge(p.sac.sem, a1),
                        v.tensor_tensor(PsS[:, i2c * T + tt * 512: i2c * T + tt * 512 + 512],
                                        psB[b][:], gsc[:],
                                        op=ALU.mult).then_inc(p.sev.sem, 1)))
            sch.run()
            sch = Sched(nc)
            ot_hist = [0, 0]
            for i in range(NT):
                bb = i % 2
                x_t = p.sxp[bb].inc(16)
                ep_ = p.sev.n
                sch.add("sync", lambda s, i=i, bb=bb, ep=ep_, ot=ot_hist[bb], x_t=x_t: (
                    s.wait_ge(p.sev.sem, ep),
                    s.wait_ge(p.sop[bb].sem, ot),
                    s.dma_start(rb2[bb][:], rtile(p.r_b, i)).then_inc(p.sxp[bb].sem, 16)))
                for ch in range(2):
                    b = (i * 2 + ch) % 2
                    fa = p.sev.n
                    m1 = p.smm.inc()
                    def dmf(t, i=i, ch=ch, b=b, fa=fa, w_t=w_t):
                        t.wait_ge(p.sev.sem, fa)
                        t.wait_ge(p.scc.sem, w_t)
                        for i2c in range(8):
                            ins = t.matmul(psA[b][:],
                                           PsS[:, i2c * T + i * 128: i2c * T + (i + 1) * 128],
                                           shd[:, i2c * 1024 + ch * 512: i2c * 1024 + ch * 512 + 512],
                                           start=(i2c == 0), stop=(i2c == 7))
                        ins.then_inc(p.smm.sem, 1)
                    sch.add("tensor", dmf)
                    p.sev.inc()
                    sch.add("vector", lambda v, ch=ch, b=b, bb=bb, m1=m1, x_t=x_t,
                            ot=ot_hist[bb]: (
                        v.wait_ge(p.smm.sem, m1),
                        v.wait_ge(p.sxp[bb].sem, x_t),
                        v.wait_ge(p.sop[bb].sem, ot),
                        v.tensor_tensor(rob[bb][:, ch * 512:(ch + 1) * 512], psA[b][:],
                                        rb2[bb][:, ch * 512:(ch + 1) * 512],
                                        op=ALU.add).then_inc(p.sev.sem, 1)))
                o_t = p.sop[bb].inc(16)
                lv = p.sev.n
                sch.add("sync", lambda s, i=i, bb=bb, lv=lv: (
                    s.wait_ge(p.sev.sem, lv),
                    s.dma_start(rtile(r_out, i), rob[bb][:]).then_inc(p.sop[bb].sem, 16)))
                ot_hist[bb] = o_t
            sch.add("sync", lambda s, n0=p.sop[0].n, n1=p.sop[1].n: (
                s.wait_ge(p.sop[0].sem, n0), s.wait_ge(p.sop[1].sem, n1)))
            sch.run()


def build_program(stages):
    nc = bass.Bass("TRN2", target_bir_lowering=False)
    with ExitStack() as ctx:
        p = Prog(nc, ctx)
        p.load_consts()
        if stages >= 4:
            queue_wc_cast(p)
        chain = [p.x, p.r_a, p.r_b, p.r_a]
        outs = [chain[i + 1] if i < stages - 1 else p.out for i in range(stages)]
        emit_l0_attn(p, p.x, outs[0])
        if stages >= 2:
            emit_l0_ffn(p, outs[0], outs[1])
        if stages >= 3:
            emit_l1_attn(p, outs[1], outs[2])
        if stages >= 4:
            emit_l1_ffn(p, outs[2], outs[3])
    return nc


def kernel(**inp):
    shared = _host_inputs(inp)
    x = inp["L0_attn_x"]
    nc = build_program(STAGES)
    in_maps = []
    for b in range(8):
        m = dict(shared)
        m["x"] = np.ascontiguousarray(x[b])
        in_maps.append(m)
    res = run_bass_kernel_spmd(nc, in_maps, core_ids=list(range(8)))
    if DEBUG:
        return res.results
    return np.stack([r["out"] for r in res.results], axis=0)


# version 15 (final)
# speedup vs baseline: 1.0615x; 1.0615x over previous
import numpy as np
import ml_dtypes
import concourse.bass as bass
import concourse.mybir as mybir
from concourse.bass_utils import run_bass_kernel_spmd
from contextlib import ExitStack

F32 = mybir.dt.float32
BF16 = mybir.dt.bfloat16
I32 = mybir.dt.int32
AX = mybir.AxisListType.X
AC = mybir.AxisListType.C
AF = mybir.ActivationFunctionType
ALU = mybir.AluOpType
BF = ml_dtypes.bfloat16
AP = bass.AP

T, C, H = 2048, 1024, 16
NT = T // 128
NCC = C // 128
STAGES = 4
DEBUG = False
NEG = -1e30
ISQ96 = 1.0 / float(np.sqrt(np.float64(96.0)))


_UID = [0]


def un(base):
    _UID[0] += 1
    return f"{base}_{_UID[0]}"


class Ctr:
    def __init__(self, sem):
        self.sem = sem
        self.n = 0

    def inc(self, k=1):
        self.n += k
        return self.n


class Sched:

    def __init__(self, nc):
        self.nc = nc
        self.q = {"tensor": [], "vector": [], "scalar": [], "gpsimd": [], "sync": []}

    def add(self, eng, fn):
        self.q[eng].append(fn)

    def run(self):
        nc = self.nc
        with nc.Block() as blk:
            if self.q["sync"]:
                @blk.sync
                def _(e, _q=self.q["sync"]):
                    for fn in _q:
                        fn(e)
            if self.q["gpsimd"]:
                @blk.gpsimd
                def _(e, _q=self.q["gpsimd"]):
                    for fn in _q:
                        fn(e)
            if self.q["tensor"]:
                @blk.tensor
                def _(e, _q=self.q["tensor"]):
                    for fn in _q:
                        fn(e)
            if self.q["vector"]:
                @blk.vector
                def _(e, _q=self.q["vector"]):
                    for fn in _q:
                        fn(e)
            if self.q["scalar"]:
                @blk.scalar
                def _(e, _q=self.q["scalar"]):
                    for fn in _q:
                        fn(e)



def _host_tables():
    t = np.arange(T, dtype=np.float64)
    inv0 = 1.0 / (10000.0 ** (np.arange(0, 32, 2, dtype=np.float64) / 32.0))
    fr0 = np.outer(t, inv0)
    emb = np.concatenate([fr0, fr0], axis=1)
    scale0 = (np.cos(emb) + np.sin(emb)).T
    ropeq0 = np.tile(scale0 * ISQ96, (4, 1)).astype(np.float32)
    ropek0 = scale0.astype(np.float32)
    inv1 = 1.0 / (10000.0 ** (np.arange(0, 64, 2, dtype=np.float64) / 64.0))
    fr1 = np.outer(t, inv1)
    cosT, sinT = np.cos(fr1).T, np.sin(fr1).T
    cq1 = np.tile(cosT * 0.125, (4, 1)).astype(np.float32)
    sq1 = np.tile(sinT * 0.125, (4, 1)).astype(np.float32)
    ck1 = np.tile(cosT, (4, 1)).astype(np.float32)
    sk1 = np.tile(sinT, (4, 1)).astype(np.float32)
    q = np.arange(128)
    maskA = np.where(q[None, :] <= q[:, None], 0.0, NEG).astype(np.float32)
    mb = np.zeros((128, 4 * 512), dtype=np.float32)
    for r in range(4):
        k = np.arange(128)[:, None]
        qq = np.arange(512)[None, :]
        mb[:, r * 512:(r + 1) * 512] = np.where(qq >= r * 128 + k, 0.0, NEG)
    ident = np.eye(128, dtype=BF)
    return dict(ropeq0=ropeq0, ropek0=ropek0, cq1=cq1, sq1=sq1, ck1=ck1, sk1=sk1,
                maskA=maskA, maskB=mb, ident=ident)


def _tile8(w):
    return w.reshape(8, 128, 1024).transpose(1, 0, 2).reshape(128, 8192)


def _host_inputs(inp):
    h = _host_tables()
    P = np.ascontiguousarray
    perm = ([hh * 64 + 2 * j for hh in range(16) for j in range(32)]
            + [hh * 64 + 2 * j + 1 for hh in range(16) for j in range(32)])
    m = {
        "wdqT": P(inp["L0_attn_W_DQ"].T),
        "wuq": P(inp["L0_attn_W_UQ"]),
        "wqr": P(inp["L0_attn_W_QR"]),
        "wdkv": P(inp["L0_attn_W_DKV"]),
        "wkr": P(inp["L0_attn_W_KR"]),
        "wuk": P(inp["L0_attn_W_UK"]),
        "wuv": P(inp["L0_attn_W_UV"]),
        "wo0": P(inp["L0_attn_W_O"]),
        "fw": P(inp["L0_ffn_weight"]),
        "gma": P(np.broadcast_to(inp["L0_ffn_ln_gamma"], (128, 1024))),
        "bta": P(np.broadcast_to(inp["L0_ffn_ln_beta"], (128, 1024))),
        "wqT1": P(inp["L1_attn_Wq"][perm, :].T),
        "wkT1": P(inp["L1_attn_Wk"][perm, :].T),
        "wvT1": P(inp["L1_attn_Wv"].T),
        "woT1": P(inp["L1_attn_Wo"].T),
        "gwT": P(inp["L1_ffn_gate_w"].T),
        "gbias": P(np.broadcast_to(inp["L1_ffn_gate_bias"], (128, 32))),
        "wgT": P(inp["L1_ffn_w_gate"].transpose(0, 2, 1)),
        "wuT": P(inp["L1_ffn_w_up"].transpose(0, 2, 1)),
        "wdT": P(inp["L1_ffn_w_down"].transpose(0, 2, 1)),
        "shgT": P(_tile8(inp["L1_ffn_sh_gate"].T)),
        "shuT": P(_tile8(inp["L1_ffn_sh_up"].T)),
        "shdT": P(_tile8(inp["L1_ffn_sh_down"].T)),
    }
    m.update(h)
    return m


class Prog:
    def __init__(self, nc, ctx):
        self.nc = nc
        e = ctx.enter_context
        dt = nc.dram_tensor
        self.x = dt("x", [T, C], F32, kind="ExternalInput")
        self.out = dt("out", [T, C], F32, kind="ExternalOutput")
        names = [("wdqT", [4096, 1024]), ("wuq", [4096, 1024]), ("wqr", [4096, 512]),
                 ("wdkv", [1024, 256]), ("wkr", [1024, 32]), ("wuk", [256, 1024]),
                 ("wuv", [256, 1024]), ("wo0", [1024, 1024]), ("fw", [1024, 1024]),
                 ("gma", [128, 1024]), ("bta", [128, 1024]),
                 ("wqT1", [1024, 1024]), ("wkT1", [1024, 1024]),
                 ("wvT1", [1024, 1024]), ("woT1", [1024, 1024]),
                 ("gwT", [1024, 32]), ("gbias", [128, 32]),
                 ("wgT", [32, 1024, 512]), ("wuT", [32, 1024, 512]),
                 ("wdT", [32, 512, 1024]),
                 ("shgT", [128, 8192]), ("shuT", [128, 8192]), ("shdT", [128, 8192]),
                 ("ropeq0", [128, T]), ("ropek0", [32, T]),
                 ("cq1", [128, T]), ("sq1", [128, T]), ("ck1", [128, T]), ("sk1", [128, T]),
                 ("maskA", [128, 128]), ("maskB", [128, 2048])]
        for n, s in names:
            setattr(self, n, dt(n, s, F32, kind="ExternalInput"))
        self.ident_in = dt("ident", [128, 128], BF16, kind="ExternalInput")
        self.dbg = {}
        if DEBUG:
            for n, sh, d in [("d_xnF", [128, 16384], BF16), ("d_qC", [128, 16384], BF16),
                             ("d_qR", [128, 16384], BF16), ("d_kR", [128, 2048], BF16),
                             ("d_cKV", [128, 4096], BF16), ("d_kpk", [128, 2048], BF16),
                             ("d_vtok", [128, 2048], BF16), ("d_OF", [128, 16384], BF16),
                             ("d_mg", [128, 2], F32), ("d_lrow", [1, 512], F32),
                             ("d_negMbc", [128, 1], F32), ("d_lbc", [128, 512], F32)]:
                self.dbg[n] = dt(n, sh, d, kind="ExternalOutput")
        self.r_a = dt("r_a", [T, C], F32)
        self.r_b = dt("r_b", [T, C], F32)
        self.cc_in = dt("cc_in", [1, 8], F32)
        self.cc_out = dt("cc_out", [1, 8], F32)
        self.wc_g = dt("wc_g", [32 * 1024, 512], BF16)
        self.wc_u = dt("wc_u", [32 * 1024, 512], BF16)
        self.wc_d = dt("wc_d", [32 * 512, 1024], BF16)
        self.wc_join = 0
        self.rep_t = 0
        self.drip = []
        self.sx = Ctr(e(nc.semaphore("sx")))
        self.sw = Ctr(e(nc.semaphore("sw")))
        self.smm = Ctr(e(nc.semaphore("smm")))
        self.sev = Ctr(e(nc.semaphore("sev")))
        self.sac = Ctr(e(nc.semaphore("sac")))
        self.sg = Ctr(e(nc.semaphore("sg")))
        self.so = Ctr(e(nc.semaphore("so")))
        self.scc = Ctr(e(nc.semaphore("scc")))
        self.sxp = [Ctr(e(nc.semaphore(f"sxp{k}"))) for k in range(2)]
        self.sop = [Ctr(e(nc.semaphore(f"sop{k}"))) for k in range(2)]
        self.swp = [Ctr(e(nc.semaphore(f"swp{k}"))) for k in range(2)]
        self.sfp = [Ctr(e(nc.semaphore(f"sfp{k}"))) for k in range(2)]
        self.sdp = [Ctr(e(nc.semaphore(f"sdp{k}"))) for k in range(2)]
        self.identity = e(nc.sbuf_tensor(un("identity"), [128, 128], BF16))
        self.ones_bf = e(nc.sbuf_tensor(un("ones_bf"), [128, 1], BF16))
        self.ones_r = e(nc.sbuf_tensor(un("ones_r"), [1, 128], F32))
        self.mAf = e(nc.sbuf_tensor(un("mAf"), [128, 128], F32))
        self.mBf = e(nc.sbuf_tensor(un("mBf"), [128, 2048], F32))
        self.idf = e(nc.sbuf_tensor(un("idfg"), [128, 128], F32))

    def take_drip(self, sch, n=1):
        for _ in range(n):
            if self.drip:
                sch.add("gpsimd", self.drip.pop(0))

    def load_consts(self):
        nc = self.nc
        t0 = self.sx.inc(16)
        tw = self.sw.inc(32)
        with nc.Block() as blk:
            @blk.sync
            def _(s):
                s.dma_start(self.identity[:], self.ident_in.ap()).then_inc(self.sx.sem, 16)
                s.wait_ge(self.sx.sem, t0)

            @blk.gpsimd
            def _(g):
                g.dma_start(self.mAf[:], self.maskA.ap()).then_inc(self.sw.sem, 16)
                g.dma_start(self.mBf[:], self.maskB.ap()).then_inc(self.sw.sem, 16)
                g.wait_ge(self.sw.sem, tw)

            @blk.vector
            def _(v):
                v.memset(self.ones_bf[:], 1.0)
                v.memset(self.ones_r[:], 1.0)
                v.wait_ge(self.sx.sem, self.sx.n)
                v.tensor_copy(self.idf[:], self.identity[:])


def queue_wc_cast(p):
    flat = [[65536, 16], [1, 65536]]
    chunks = []
    for dst, src in ((p.wc_g, p.wgT), (p.wc_u, p.wuT), (p.wc_d, p.wdT)):
        for i in range(16):
            chunks.append((dst, src, i * 1048576))
    p.wc_join = p.swp[0].inc(16 * len(chunks))

    def mk(dst, src, off):
        return lambda g: g.dma_start(AP(dst, off, flat),
                                     AP(src, off, flat)).then_inc(p.swp[0].sem, 16)
    p.drip = [mk(*c) for c in chunks]


def emit_taps(p, pairs):
    if not DEBUG:
        return
    nc = p.nc
    sch = Sched(nc)
    waits = (p.sev.n, p.smm.n, p.sac.n, p.sg.n)
    def f(s, waits=waits, pairs=list(pairs)):
        s.wait_ge(p.sev.sem, waits[0])
        s.wait_ge(p.smm.sem, waits[1])
        s.wait_ge(p.sac.sem, waits[2])
        s.wait_ge(p.sg.sem, waits[3])
        for name, src in pairs:
            s.dma_start(p.dbg[name].ap(), src).then_inc(p.so.sem, 16)
            p.so.inc(16)
        s.wait_ge(p.so.sem, p.so.n)
    sch.add("sync", f)
    sch.run()


def hbm_tiled_ap(t, rows, cols):
    return AP(t, 0, [[cols, 128], [128 * cols, rows // 128], [1, cols]])


def rtile(t, i):
    return AP(t, i * 128 * C, [[C, 128], [1, C]])



def emit_ln(p, ctx, r_in, xnF, keep_tok=None, per_tile_cb=None):
    nc = p.nc
    e = ctx.enter_context
    rbuf = [e(nc.sbuf_tensor(un(f"ln_r{k}"), [128, C], F32)) for k in range(2)]
    xbuf = [e(nc.sbuf_tensor(un(f"ln_x{k}"), [128, C], BF16)) for k in range(2)]
    sqs = e(nc.sbuf_tensor(un("ln_sq"), [128, C], F32))
    st = [e(nc.sbuf_tensor(un(f"ln_st{k}"), [128, 8], F32)) for k in range(2)]
    ps_tr = [e(nc.psum_tensor(un(f"ln_pt{k}"), [128, 128], BF16)) for k in range(2)]
    sch = Sched(nc)
    a3_hist = [0, 0]
    rb_sev = [0, 0]
    tr_hist = [0, 0]
    trb_free = [0, 0]
    for i in range(NT):
        b = i % 2
        s = st[b]
        xc = p.sxp[b]
        x_t = xc.inc(16)
        w_r, w_x = a3_hist[b], tr_hist[b]
        w_v = rb_sev[b]
        sch.add("sync", lambda sy, i=i, b=b, w_r=w_r, w_x=w_x, w_v=w_v: (
            sy.wait_ge(p.sac.sem, w_r),
            sy.wait_ge(p.smm.sem, w_x),
            sy.wait_ge(p.sev.sem, w_v),
            sy.dma_start(rbuf[b][:], rtile(r_in, i)).then_inc(p.sxp[b].sem, 16)))
        v1 = p.sev.inc()
        sch.add("vector", lambda v, b=b, s=s, x_t=x_t: (
            v.wait_ge(p.sxp[b].sem, x_t),
            v.reduce_sum(s[:, 0:1], rbuf[b][:], axis=AX).then_inc(p.sev.sem, 1)))
        a1 = p.sac.inc()
        sch.add("scalar", lambda a, b=b, s=s, x_t=x_t: (
            a.wait_ge(p.sxp[b].sem, x_t),
            a.activation(sqs[:], rbuf[b][:], AF.Square).then_inc(p.sac.sem, 1)))
        v2 = p.sev.inc()
        sch.add("vector", lambda v, s=s, a1=a1: (
            v.wait_ge(p.sac.sem, a1),
            v.reduce_sum(s[:, 1:2], sqs[:], axis=AX),
            v.drain(),
            v.tensor_scalar(s[:, 2:3], s[:, 0:1], 1.0 / C, None, op0=ALU.mult),
            v.drain(),
            v.tensor_tensor(s[:, 3:4], s[:, 2:3], s[:, 2:3], op=ALU.mult),
            v.tensor_scalar(s[:, 4:5], s[:, 1:2], 1.0 / C, None, op0=ALU.mult),
            v.drain(),
            v.tensor_tensor(s[:, 3:4], s[:, 4:5], s[:, 3:4], op=ALU.subtract),
            v.drain(),
            v.tensor_scalar(s[:, 3:4], s[:, 3:4], 1e-5, None,
                            op0=ALU.add).then_inc(p.sev.sem, 1)))
        a2 = p.sac.inc()
        sch.add("scalar", lambda a, s=s, v2=v2: (
            a.wait_ge(p.sev.sem, v2),
            a.activation(s[:, 4:5], s[:, 3:4], AF.Sqrt).then_inc(p.sac.sem, 1)))
        v3 = p.sev.inc()
        sch.add("vector", lambda v, s=s, a2=a2: (
            v.wait_ge(p.sac.sem, a2),
            v.reciprocal(s[:, 5:6], s[:, 4:5]),
            v.drain(),
            v.scalar_tensor_tensor(s[:, 6:7], s[:, 2:3], -1.0, s[:, 5:6],
                                   op0=ALU.mult, op1=ALU.mult).then_inc(p.sev.sem, 1)))
        a3 = p.sac.inc()
        sch.add("scalar", lambda a, b=b, s=s, v3=v3: (
            a.wait_ge(p.sev.sem, v3),
            a.activation(xbuf[b][:], rbuf[b][:], AF.Identity, bias=s[:, 6:7],
                         scale=s[:, 5:6]).then_inc(p.sac.sem, 1)))
        a3_hist[b] = a3
        if keep_tok is not None:
            v3b = p.sev.inc()
            sch.add("vector", lambda v, b=b, i=i, s=s, v3=v3: (
                v.wait_ge(p.sev.sem, v3),
                v.drain(),
                v.scalar_tensor_tensor(keep_tok[:, i * C:(i + 1) * C], rbuf[b][:],
                                       s[:, 5:6], AP(s, 6, [[8, 128], [0, C]]),
                                       op0=ALU.mult,
                                       op1=ALU.add).then_inc(p.sev.sem, 1)))
            rb_sev[b] = v3b
        last_m = 0
        for cc in range(NCC if xnF is not None else 0):
            tb = cc % 2
            m1 = p.smm.inc()
            fr = trb_free[tb]
            sch.add("tensor", lambda t, b=b, cc=cc, tb=tb, a3=a3, fr=fr: (
                t.wait_ge(p.sac.sem, a3),
                t.wait_ge(p.sev.sem, fr),
                t.transpose(ps_tr[tb][:], xbuf[b][:, cc * 128:(cc + 1) * 128],
                            p.identity[:]).then_inc(p.smm.sem, 1)))
            v4 = p.sev.inc()
            sch.add("vector", lambda v, i=i, cc=cc, tb=tb, m1=m1: (
                v.wait_ge(p.smm.sem, m1),
                v.tensor_copy(xnF[:, cc * T + i * 128: cc * T + (i + 1) * 128],
                              ps_tr[tb][:]).then_inc(p.sev.sem, 1)))
            trb_free[tb] = v4
            last_m = m1
        tr_hist[b] = last_m
        if per_tile_cb is not None:
            per_tile_cb(sch, i, rbuf[b], st[b], a3)
    sch.run()




def attn_core(p, nc, ps, ps_o, ps_l, ps_bc, PT, mg, negM, negMbc,
              lrow, linv, lbc, heads, act_prev0):
    sch = Sched(nc)
    act_prev = act_prev0
    rank1s = []
    for g in range(4):
        g_prev = p.smm.n
        nch = 0
        for qt in range(4):
            qi = g * 4 + qt
            for kc in range(0, (qi + 1) * 128, 512):
                n = min(512, (qi + 1) * 128 - kc)
                isdiag = (kc + n == (qi + 1) * 128)
                for hi, hd in enumerate(heads):
                    fa = p.sev.n
                    m1 = p.smm.inc()
                    sch.add("tensor", lambda t, hd=hd, qi=qi, kc=kc, n=n, hi=hi, fa=fa: (
                        t.wait_ge(p.sev.sem, fa),
                        hd["mmA"](t, hi, qi, kc, n, True)))
                    p.sev.inc()
                    def evf(v, hi=hi, n=n, m1=m1, mcol=hi * 17 + nch, fc=(nch == 0),
                            isdiag=isdiag):
                        v.wait_ge(p.smm.sem, m1)
                        if fc or isdiag:
                            v.drain()
                        if isdiag:
                            v.tensor_tensor(ps[hi][:, n - 128:n], ps[hi][:, n - 128:n],
                                            p.mAf[:], op=ALU.add)
                            v.drain()
                        v.reduce_max(mg[:, mcol:mcol + 1], ps[hi][:, 0:n],
                                     axis=AX).then_inc(p.sev.sem, 1)
                    sch.add("vector", evf)
                nch += 1
        for hi in range(2):
            col = hi * 4 + g
            v9 = p.sev.inc()
            sch.add("vector", lambda v, nch=nch, gp=g_prev, hi=hi: (
                v.wait_ge(p.smm.sem, gp),
                v.drain(),
                v.reduce_max(mg[:, 38 + hi:39 + hi], mg[:, hi * 17:hi * 17 + nch],
                             axis=AX).then_inc(p.sev.sem, 1)))
            m_tr = p.smm.inc()
            sch.add("tensor", lambda t, v9=v9, hi=hi: (
                t.wait_ge(p.sev.sem, v9),
                t.transpose(ps_bc[0:1, 384:512], mg[:, 38 + hi:39 + hi],
                            p.idf[:]).then_inc(p.smm.sem, 1)))
            g1 = p.sev.inc()
            sch.add("vector", lambda v, m_tr=m_tr, col=col: (
                v.wait_ge(p.smm.sem, m_tr),
                v.reduce_max(negM[0:1, col:col + 1], ps_bc[0:1, 384:512], axis=AX,
                             negate=True).then_inc(p.sev.sem, 1)))
            rank1s.append((g1, col))
    m_rk = p.smm.inc(len(rank1s))
    def rkf(t, rank1s=list(rank1s)):
        t.wait_ge(p.sev.sem, rank1s[-1][0])
        for g1, col in rank1s:
            t.matmul(ps_bc[:, col:col + 1], p.ones_r[:], negM[0:1, col:col + 1],
                     start=True, stop=True).then_inc(p.smm.sem, 1)
    sch.add("tensor", rkf)
    p.sev.inc()
    sch.add("vector", lambda v, m_rk=m_rk, ap=act_prev: (
        v.wait_ge(p.smm.sem, m_rk),
        v.wait_ge(p.sac.sem, ap),
        v.tensor_copy(negMbc[:, 0:8], ps_bc[:, 0:8]).then_inc(p.sev.sem, 1)))
    negm_ev = p.sev.n
    for g in range(4):
        po_free = p.sev.n
        nkb = 4 * (g + 1)
        for kb in range(nkb):
            r = kb - 4 * g
            deps = []
            for hi, hd in enumerate(heads):
                free_at = p.sac.n
                m1 = p.smm.inc()
                sch.add("tensor", lambda t, hd=hd, g=g, kb=kb, hi=hi, fr=free_at: (
                    t.wait_ge(p.sac.sem, fr),
                    hd["mmB"](t, hi, g, kb, True)))
                if r >= 0:
                    dv = p.sev.inc()
                    sch.add("vector", lambda v, hi=hi, r=r, m1=m1: (
                        v.wait_ge(p.smm.sem, m1),
                        v.drain(),
                        v.tensor_tensor(ps[hi][:], ps[hi][:],
                                        p.mBf[:, r * 512:(r + 1) * 512],
                                        op=ALU.add).then_inc(p.sev.sem, 1)))
                    deps.append((p.sev.sem, dv))
                else:
                    deps.append((p.smm.sem, m1))
            a1s = []
            for hi, hd in enumerate(heads):
                col = hi * 4 + g
                pb = hi * 2 + kb % 2
                pt_free = p.smm.n
                a1 = p.sac.inc()
                sch.add("scalar", lambda a, hi=hi, pb=pb, ds=deps[hi][0],
                        dv=deps[hi][1], ne=negm_ev, pf=pt_free, col=col: (
                    a.wait_ge(ds, dv),
                    a.wait_ge(p.sev.sem, ne),
                    a.wait_ge(p.smm.sem, pf),
                    a.activation(PT[pb][:], ps[hi][:], AF.Exp,
                                 bias=negMbc[:, col:col + 1]).then_inc(p.sac.sem, 1)))
                a1s.append(a1)
            for hi, hd in enumerate(heads):
                pb = hi * 2 + kb % 2
                m2 = p.smm.inc(2)
                def mmf2(t, hd=hd, hi=hi, kb=kb, pb=pb, a1=a1s[hi], nkb=nkb,
                         pf=po_free):
                    t.wait_ge(p.sac.sem, a1)
                    if kb == 0:
                        t.wait_ge(p.sev.sem, pf)
                    t.matmul(ps_l[hi][0:1, :], p.ones_bf[:], PT[pb][:],
                             start=(kb == 0), stop=(kb == nkb - 1)
                             ).then_inc(p.smm.sem, 1)
                    t.matmul(ps_o[hi][0:64, :], hd["pv_lhs"](kb), PT[pb][:],
                             start=(kb == 0), stop=(kb == nkb - 1)
                             ).then_inc(p.smm.sem, 1)
                sch.add("tensor", mmf2)
        m_last = p.smm.n
        for hi, hd in enumerate(heads):
            v1 = p.sev.inc()
            sch.add("vector", lambda v, ml=m_last, hi=hi: (
                v.wait_ge(p.smm.sem, ml),
                v.tensor_copy(lrow[:], ps_l[hi][0:1, :]),
                v.drain(),
                v.reciprocal(linv[:], lrow[:]).then_inc(p.sev.sem, 1)))
            m1 = p.smm.inc()
            sch.add("tensor", lambda t, v1=v1: (
                t.wait_ge(p.sev.sem, v1),
                t.matmul(ps_bc[:], p.ones_r[:], linv[:],
                         start=True, stop=True).then_inc(p.smm.sem, 1)))
            p.sev.inc()
            def evo(v, hd=hd, hi=hi, g=g, m1=m1):
                v.wait_ge(p.smm.sem, m1)
                v.tensor_copy(lbc[:], ps_bc[:])
                v.drain()
                v.tensor_tensor(hd["of_dst"](g), ps_o[hi][0:64, :], lbc[0:64, :],
                                op=ALU.mult).then_inc(p.sev.sem, 1)
            sch.add("vector", evo)
    p.take_drip(sch, 3)
    sch.run()


def emit_l0_attn(p, r_in, r_out):
    nc = p.nc
    with ExitStack() as ctx:
        e = ctx.enter_context
        ps = [e(nc.psum_tensor(un(f"l0ps{k}"), [128, 512], F32)) for k in range(2)]
        wqe = e(nc.sbuf_tensor(un("wqe"), [128, 8 * 1024], BF16))
        wqre = e(nc.sbuf_tensor(un("wqre"), [128, 8 * 512], BF16))
        with ExitStack() as c2:
            e2 = c2.enter_context
            wdq = e2(nc.sbuf_tensor(un("wdq"), [128, 32 * 1024], BF16))
            wuqs = e2(nc.sbuf_tensor(un("wuqs"), [128, 32 * 1024], BF16))
            wqrs = e2(nc.sbuf_tensor(un("wqrs"), [128, 32 * 512], BF16))
            pse = [e2(nc.psum_tensor(un(f"pse{k}"), [128, 512], F32)) for k in range(2)]
            sch = Sched(nc)
            w_t = p.sw.inc(48)
            sch.add("gpsimd", lambda g: (
                g.dma_start(wdq[:], hbm_tiled_ap(p.wdqT, 4096, 1024)).then_inc(p.sw.sem, 16),
                g.dma_start(wuqs[:], hbm_tiled_ap(p.wuq, 4096, 1024)).then_inc(p.sw.sem, 16),
                g.dma_start(wqrs[:], hbm_tiled_ap(p.wqr, 4096, 512)).then_inc(p.sw.sem, 16)))
            jobs = [("q", cc, oh) for cc in range(8) for oh in range(2)]
            jobs += [("r", cc, 0) for cc in range(8)]
            for bi, (kind, cc, oh) in enumerate(jobs):
                b = bi % 2
                fa = p.sev.n
                m1 = p.smm.inc()
                def mmf(t, kind=kind, cc=cc, oh=oh, b=b, w_t=w_t, fa=fa):
                    t.wait_ge(p.sw.sem, w_t)
                    t.wait_ge(p.sev.sem, fa)
                    for uc in range(32):
                        lhs = wdq[:, uc * 1024 + cc * 128: uc * 1024 + (cc + 1) * 128]
                        if kind == "q":
                            rhs = wuqs[:, uc * 1024 + oh * 512: uc * 1024 + oh * 512 + 512]
                        else:
                            rhs = wqrs[:, uc * 512:(uc + 1) * 512]
                        ins = t.matmul(pse[b][:], lhs, rhs, start=(uc == 0), stop=(uc == 31))
                    ins.then_inc(p.smm.sem, 1)
                sch.add("tensor", mmf)
                p.sev.inc()
                def evf(v, kind=kind, cc=cc, oh=oh, b=b, m1=m1):
                    v.wait_ge(p.smm.sem, m1)
                    if kind == "q":
                        dst = wqe[:, cc * 1024 + oh * 512: cc * 1024 + oh * 512 + 512]
                    else:
                        dst = wqre[:, cc * 512:(cc + 1) * 512]
                    v.tensor_copy(dst, pse[b][:]).then_inc(p.sev.sem, 1)
                sch.add("vector", evf)
            sch.run()
        cKV = e(nc.sbuf_tensor(un("cKV"), [128, 2 * T], BF16))
        kR = e(nc.sbuf_tensor(un("kR"), [128, T], BF16))
        qC = e(nc.sbuf_tensor(un("qC"), [128, 8 * T], BF16))
        qR = e(nc.sbuf_tensor(un("qR"), [128, 8 * T], BF16))
        wuks = e(nc.sbuf_tensor(un("wuks"), [128, 2 * 1024], BF16))
        wuvs = e(nc.sbuf_tensor(un("wuvs"), [128, 2 * 1024], BF16))
        kpk = e(nc.sbuf_tensor(un("kpk"), [128, T], BF16))
        vtok = e(nc.sbuf_tensor(un("vtok"), [128, 16 * 128], BF16))
        PT = [e(nc.sbuf_tensor(un(f"PT{k}"), [128, 512], BF16)) for k in range(4)]
        mg = e(nc.sbuf_tensor(un("mg"), [128, 40], F32))
        negM = e(nc.sbuf_tensor(un("negM"), [1, 8], F32))
        negMbc = e(nc.sbuf_tensor(un("negMbc"), [128, 8], F32))
        act_prev = [0]
        lrow = e(nc.sbuf_tensor(un("lrow"), [1, 512], F32))
        linv = e(nc.sbuf_tensor(un("linv"), [1, 512], F32))
        lbc = e(nc.sbuf_tensor(un("lbc"), [128, 512], F32))

        cproj = ctx.enter_context(ExitStack())
        ep = cproj.enter_context
        xnF = ep(nc.sbuf_tensor(un("xnF0"), [128, 8 * T], BF16))
        wdkvs = ep(nc.sbuf_tensor(un("wdkvs"), [128, 8 * 256], BF16))
        wkrs = ep(nc.sbuf_tensor(un("wkrs"), [128, 8 * 32], BF16))
        rq0 = ep(nc.sbuf_tensor(un("rq0"), [128, T], F32))
        rk0 = ep(nc.sbuf_tensor(un("rk0"), [32, T], F32))
        emit_ln(p, cproj, r_in, xnF)

        sch = Sched(nc)
        w_t = p.sw.inc(16 * 5)
        sch.add("gpsimd", lambda g: (
            g.dma_start(wdkvs[:], hbm_tiled_ap(p.wdkv, 1024, 256)).then_inc(p.sw.sem, 16),
            g.dma_start(wkrs[:], hbm_tiled_ap(p.wkr, 1024, 32)).then_inc(p.sw.sem, 16),
            g.dma_start(wuks[:], hbm_tiled_ap(p.wuk, 256, 1024)).then_inc(p.sw.sem, 16),
            g.dma_start(wuvs[:], hbm_tiled_ap(p.wuv, 256, 1024)).then_inc(p.sw.sem, 16),
            g.dma_start(rq0[:], p.ropeq0.ap()).then_inc(p.sw.sem, 16)))
        x_t = p.sx.inc(16)
        sch.add("sync", lambda s: (
            s.dma_start(rk0[:], p.ropek0.ap()).then_inc(p.sx.sem, 16)))
        p.sev.inc()
        sch.add("vector", lambda v: (
            v.memset(qR[32:64, :], 0.0),
            v.memset(qR[96:128, :], 0.0),
            v.memset(kR[32:64, :], 0.0),
            v.memset(kR[96:128, :], 0.0).then_inc(p.sev.sem, 1)))

        for dcc in range(2):
            for tt in range(4):
                b = (dcc * 4 + tt) % 2
                fa = p.sev.n
                m1 = p.smm.inc()
                def mmf(t, dcc=dcc, tt=tt, b=b, fa=fa, w_t=w_t):
                    t.wait_ge(p.sw.sem, w_t)
                    t.wait_ge(p.sev.sem, fa)
                    for cc in range(8):
                        ins = t.matmul(ps[b][:],
                                       wdkvs[:, cc * 256 + dcc * 128: cc * 256 + dcc * 128 + 128],
                                       xnF[:, cc * T + tt * 512: cc * T + tt * 512 + 512],
                                       start=(cc == 0), stop=(cc == 7))
                    ins.then_inc(p.smm.sem, 1)
                sch.add("tensor", mmf)
                p.sev.inc()
                sch.add("vector", lambda v, dcc=dcc, tt=tt, b=b, m1=m1: (
                    v.wait_ge(p.smm.sem, m1),
                    v.tensor_copy(cKV[:, dcc * T + tt * 512: dcc * T + tt * 512 + 512],
                                  ps[b][:]).then_inc(p.sev.sem, 1)))
        for tt in range(4):
            b = tt % 2
            fa = p.sev.n
            m1 = p.smm.inc()
            def mmf(t, tt=tt, b=b, fa=fa, x_t=x_t):
                t.wait_ge(p.sx.sem, x_t)
                t.wait_ge(p.sev.sem, fa)
                for cc in range(8):
                    ins = t.matmul(ps[b][0:32, :],
                                   wkrs[:, cc * 32:(cc + 1) * 32],
                                   xnF[:, cc * T + tt * 512: cc * T + tt * 512 + 512],
                                   start=(cc == 0), stop=(cc == 7))
                ins.then_inc(p.smm.sem, 1)
            sch.add("tensor", mmf)
            p.sev.inc()
            sch.add("vector", lambda v, tt=tt, b=b, m1=m1: (
                v.wait_ge(p.smm.sem, m1),
                v.tensor_tensor(kR[0:32, tt * 512:(tt + 1) * 512], ps[b][0:32, :],
                                rk0[0:32, tt * 512:(tt + 1) * 512],
                                op=ALU.mult).then_inc(p.sev.sem, 1)))
        kr_ev = p.sev.n
        kr_t = p.sx.inc(16)
        sch.add("sync", lambda s, kr_ev=kr_ev: (
            s.wait_ge(p.sev.sem, kr_ev),
            s.dma_start(kR[64:96, :], kR[0:32, :]).then_inc(p.sx.sem, 16)))
        for j in range(8):
            for tt in range(4):
                b = (j * 4 + tt) % 2
                fa = p.sev.n
                m1 = p.smm.inc()
                def mmf(t, j=j, tt=tt, b=b, fa=fa):
                    t.wait_ge(p.sev.sem, fa)
                    for hh in range(2):
                        h = 2 * j + hh
                        for cc in range(8):
                            ins = t.matmul(ps[b][hh * 64:hh * 64 + 64, :],
                                           wqe[:, cc * 1024 + h * 64: cc * 1024 + h * 64 + 64],
                                           xnF[:, cc * T + tt * 512: cc * T + tt * 512 + 512],
                                           start=(cc == 0), stop=(cc == 7),
                                           tile_position=(0, hh * 64))
                    ins.then_inc(p.smm.sem, 1)
                sch.add("tensor", mmf)
                p.sev.inc()
                sch.add("vector", lambda v, j=j, tt=tt, b=b, m1=m1: (
                    v.wait_ge(p.smm.sem, m1),
                    v.tensor_scalar(qC[:, j * T + tt * 512: j * T + tt * 512 + 512],
                                    ps[b][:], ISQ96, None,
                                    op0=ALU.mult).then_inc(p.sev.sem, 1)))
        for j in range(8):
            for tt in range(4):
                b = (j * 4 + tt) % 2
                fa = p.sev.n
                m1 = p.smm.inc()
                def mmf(t, j=j, tt=tt, b=b, fa=fa, w_t=w_t):
                    t.wait_ge(p.sw.sem, w_t)
                    t.wait_ge(p.sev.sem, fa)
                    for hh in range(2):
                        h = 2 * j + hh
                        for cc in range(8):
                            ins = t.matmul(ps[b][hh * 64:hh * 64 + 32, :],
                                           wqre[:, cc * 512 + h * 32: cc * 512 + h * 32 + 32],
                                           xnF[:, cc * T + tt * 512: cc * T + tt * 512 + 512],
                                           start=(cc == 0), stop=(cc == 7),
                                           tile_position=(0, hh * 64))
                    ins.then_inc(p.smm.sem, 1)
                sch.add("tensor", mmf)
                p.sev.inc()
                def evf(v, j=j, tt=tt, b=b, m1=m1):
                    v.wait_ge(p.smm.sem, m1)
                    v.tensor_tensor(qR[0:32, j * T + tt * 512: j * T + tt * 512 + 512],
                                    ps[b][0:32, :], rq0[0:32, tt * 512:(tt + 1) * 512],
                                    op=ALU.mult)
                    v.tensor_tensor(qR[64:96, j * T + tt * 512: j * T + tt * 512 + 512],
                                    ps[b][64:96, :], rq0[64:96, tt * 512:(tt + 1) * 512],
                                    op=ALU.mult).then_inc(p.sev.sem, 1)
                sch.add("vector", evf)
        kr_wait = kr_t
        sch.add("sync", lambda s: s.wait_ge(p.sx.sem, kr_wait))
        sch.run()
        emit_taps(p, [("d_xnF", xnF[:]), ("d_qC", qC[:]), ("d_qR", qR[:]),
                      ("d_kR", kR[:]), ("d_cKV", cKV[:])])
        cproj.close()

        ps_o = [e(nc.psum_tensor(un(f"l0po{k}"), [128, 512], F32)) for k in range(2)]
        ps_l = [e(nc.psum_tensor(un(f"l0pl{k}"), [128, 512], F32)) for k in range(2)]
        ps_bc = e(nc.psum_tensor(un("l0pb"), [128, 512], F32))
        OF = e(nc.sbuf_tensor(un("OF"), [128, 8 * T], BF16))
        wo0s = e(nc.sbuf_tensor(un("wo0s"), [128, 8 * 1024], BF16))
        rb2 = [e(nc.sbuf_tensor(un(f"rb2{k}"), [128, C], F32)) for k in range(2)]
        rob = [e(nc.sbuf_tensor(un(f"rob{k}"), [128, C], F32)) for k in range(2)]
        sch = Sched(nc)
        w_t = p.sw.inc(16)
        sch.add("gpsimd", lambda g: (
            g.dma_start(wo0s[:], hbm_tiled_ap(p.wo0, 1024, 1024)).then_inc(p.sw.sem, 16)))
        sch.run()

        for j in range(8):
            sch = Sched(nc)
            for tt in range(4):
                b = tt % 2
                fa = p.sev.n
                m1 = p.smm.inc()
                def mmf(t, j=j, tt=tt, b=b, fa=fa, w_t=w_t):
                    t.wait_ge(p.sw.sem, w_t)
                    t.wait_ge(p.sev.sem, fa)
                    for hh in range(2):
                        h = 2 * j + hh
                        for dcc in range(2):
                            ins = t.matmul(ps[b][hh * 64:hh * 64 + 64, :],
                                           wuks[:, dcc * 1024 + h * 64: dcc * 1024 + h * 64 + 64],
                                           cKV[:, dcc * T + tt * 512: dcc * T + tt * 512 + 512],
                                           start=(dcc == 0), stop=(dcc == 1),
                                           tile_position=(0, hh * 64))
                    ins.then_inc(p.smm.sem, 1)
                sch.add("tensor", mmf)
                p.sev.inc()
                sch.add("vector", lambda v, tt=tt, b=b, m1=m1: (
                    v.wait_ge(p.smm.sem, m1),
                    v.tensor_copy(kpk[:, tt * 512:(tt + 1) * 512],
                                  ps[b][:]).then_inc(p.sev.sem, 1)))
            for i in range(NT):
                b = i % 2
                fa = p.sev.n
                m1 = p.smm.inc()
                def mmf(t, j=j, i=i, b=b, fa=fa):
                    t.wait_ge(p.sev.sem, fa)
                    for dcc in range(2):
                        ins = t.matmul(ps[b][:, 0:128],
                                       cKV[:, dcc * T + i * 128: dcc * T + i * 128 + 128],
                                       wuvs[:, dcc * 1024 + 2 * j * 64: dcc * 1024 + 2 * j * 64 + 128],
                                       start=(dcc == 0), stop=(dcc == 1))
                    ins.then_inc(p.smm.sem, 1)
                sch.add("tensor", mmf)
                p.sev.inc()
                sch.add("vector", lambda v, i=i, b=b, m1=m1: (
                    v.wait_ge(p.smm.sem, m1),
                    v.tensor_copy(vtok[:, i * 128:(i + 1) * 128],
                                  ps[b][:, 0:128]).then_inc(p.sev.sem, 1)))
            sch.run()

            def mk_head(ro):
                def mmA(t, b, qi, kc, n, last, ro=ro, j=j):
                    t.matmul(ps[b][:, 0:n],
                             qC[ro:ro + 64, j * T + qi * 128: j * T + qi * 128 + 128],
                             kpk[ro:ro + 64, kc:kc + n],
                             start=True, stop=False, tile_position=(ro, 0))
                    ins = t.matmul(ps[b][:, 0:n],
                             qR[ro:ro + 64, j * T + qi * 128: j * T + qi * 128 + 128],
                             kR[ro:ro + 64, kc:kc + n],
                             start=False, stop=True,
                             tile_position=(ro, 0))
                    if last:
                        ins.then_inc(p.smm.sem, 1)
                def mmB(t, b, g, kb, last, ro=ro, j=j):
                    t.matmul(ps[b][:],
                             kpk[ro:ro + 64, kb * 128:(kb + 1) * 128],
                             qC[ro:ro + 64, j * T + g * 512: j * T + g * 512 + 512],
                             start=True, stop=False, tile_position=(ro, 0))
                    ins = t.matmul(ps[b][:],
                             kR[ro:ro + 64, kb * 128:(kb + 1) * 128],
                             qR[ro:ro + 64, j * T + g * 512: j * T + g * 512 + 512],
                             start=False, stop=True,
                             tile_position=(ro, 0))
                    if last:
                        ins.then_inc(p.smm.sem, 1)
                def pv_lhs(kb, ro=ro):
                    return vtok[:, kb * 128 + ro: kb * 128 + ro + 64]
                def of_dst(g, ro=ro, j=j):
                    return OF[ro:ro + 64, j * T + g * 512: j * T + g * 512 + 512]
                return dict(mmA=mmA, mmB=mmB, pv_lhs=pv_lhs, of_dst=of_dst)
            attn_core(p, nc, ps, ps_o, ps_l, ps_bc, PT, mg, negM, negMbc,
                      lrow, linv, lbc, [mk_head(0), mk_head(64)], act_prev[0])
            act_prev[0] = p.sac.n

        sch = Sched(nc)
        ot_hist = [0, 0]
        for i in range(NT):
            bb = i % 2
            x_t = p.sxp[bb].inc(16)
            ev_prev = p.sev.n
            sch.add("sync", lambda s, i=i, bb=bb, ep=ev_prev, ot=ot_hist[bb]: (
                s.wait_ge(p.sev.sem, ep),
                s.wait_ge(p.sop[bb].sem, ot),
                s.dma_start(rb2[bb][:], rtile(r_in, i)).then_inc(p.sxp[bb].sem, 16)))
            for ch in range(2):
                b = (i * 2 + ch) % 2
                fa = p.sev.n
                m1 = p.smm.inc()
                def mmf(t, i=i, ch=ch, b=b, fa=fa, w_t=w_t):
                    t.wait_ge(p.sw.sem, w_t)
                    t.wait_ge(p.sev.sem, fa)
                    for oc in range(8):
                        ins = t.matmul(ps[b][:],
                                       OF[:, oc * T + i * 128: oc * T + i * 128 + 128],
                                       wo0s[:, oc * 1024 + ch * 512: oc * 1024 + ch * 512 + 512],
                                       start=(oc == 0), stop=(oc == 7))
                    ins.then_inc(p.smm.sem, 1)
                sch.add("tensor", mmf)
                p.sev.inc()
                sch.add("vector", lambda v, ch=ch, b=b, bb=bb, m1=m1, x_t=x_t,
                        ot=ot_hist[bb]: (
                    v.wait_ge(p.smm.sem, m1),
                    v.wait_ge(p.sxp[bb].sem, x_t),
                    v.wait_ge(p.sop[bb].sem, ot),
                    v.tensor_tensor(rob[bb][:, ch * 512:(ch + 1) * 512], ps[b][:],
                                    rb2[bb][:, ch * 512:(ch + 1) * 512],
                                    op=ALU.add).then_inc(p.sev.sem, 1)))
            o_t = p.sop[bb].inc(16)
            last_v = p.sev.n
            sch.add("sync", lambda s, i=i, bb=bb, lv=last_v: (
                s.wait_ge(p.sev.sem, lv),
                s.dma_start(rtile(r_out, i), rob[bb][:]).then_inc(p.sop[bb].sem, 16)))
            ot_hist[bb] = o_t
        sch.add("sync", lambda s, n0=p.sop[0].n, n1=p.sop[1].n: (
            s.wait_ge(p.sop[0].sem, n0), s.wait_ge(p.sop[1].sem, n1)))
        sch.run()


def emit_l0_ffn(p, r_in, r_out):
    nc = p.nc
    QB, EPS, LN2E = 128.0, 1e-5, 1e-3
    with ExitStack() as ctx:
        e = ctx.enter_context
        ps = [e(nc.psum_tensor(un(f"f0ps{k}"), [128, 512], F32)) for k in range(2)]
        pst = [e(nc.psum_tensor(un(f"f0pt{k}"), [128, 128], BF16)) for k in range(2)]
        ps_bc = e(nc.psum_tensor(un("f0pb"), [128, 512], F32))
        wq = e(nc.sbuf_tensor(un("wq"), [128, 8 * 1024], BF16))
        sc = e(nc.sbuf_tensor(un("sc"), [1, 8], F32))
        bcs = e(nc.sbuf_tensor(un("bcs"), [128, 4], F32))
        gcol = e(nc.sbuf_tensor(un("gcol"), [128, 2], F32))
        stg = e(nc.sbuf_tensor(un("stg"), [1, 8], F32))

        with ExitStack() as c2:
            e2 = c2.enter_context
            fw = e2(nc.sbuf_tensor(un("fws"), [128, 8 * 1024], F32))
            fsc = e2(nc.sbuf_tensor(un("fsc"), [128, 8 * 1024], F32))
            sch = Sched(nc)
            w_t = p.sw.inc(16)
            sch.add("gpsimd", lambda g: (
                g.dma_start(fw[:], hbm_tiled_ap(p.fw, 1024, 1024)).then_inc(p.sw.sem, 16)))
            v1 = p.sev.inc()
            sch.add("vector", lambda v: (
                v.wait_ge(p.sw.sem, p.sw.n),
                v.reduce_sum(bcs[:, 3:4], fw[:], axis=AX,
                             apply_absolute_value=True).then_inc(p.sev.sem, 1)))
            mt = p.smm.inc()
            sch.add("tensor", lambda t, v1=v1: (
                t.wait_ge(p.sev.sem, v1),
                t.transpose(ps_bc[0:1, 384:512], bcs[:, 3:4],
                            p.idf[:]).then_inc(p.smm.sem, 1)))
            v2 = p.sev.inc()
            sch.add("vector", lambda v, mt=mt: (
                v.wait_ge(p.smm.sem, mt),
                v.reduce_sum(sc[:, 0:1], ps_bc[0:1, 384:512], axis=AX),
                v.drain(),
                v.tensor_scalar(sc[:, 0:1], sc[:, 0:1], 1.0 / (1024.0 * 1024.0), None,
                                op0=ALU.mult),
                v.drain(),
                v.tensor_scalar(sc[:, 5:6], sc[:, 0:1], EPS, None, op0=ALU.add),
                v.drain(),
                v.reciprocal(sc[:, 1:2], sc[:, 5:6]).then_inc(p.sev.sem, 1)))
            m1 = p.smm.inc()
            sch.add("tensor", lambda t, v2=v2: (
                t.wait_ge(p.sev.sem, v2),
                t.matmul(ps_bc[:, 0:1], p.ones_r[:], sc[:, 1:2],
                         start=True, stop=True).then_inc(p.smm.sem, 1)))
            v3 = p.sev.inc()
            sch.add("vector", lambda v, m1=m1: (
                v.wait_ge(p.smm.sem, m1),
                v.tensor_copy(bcs[:, 0:1], ps_bc[:, 0:1]),
                v.drain(),
                v.tensor_scalar(fsc[:], fw[:], bcs[:, 0:1], None,
                                op0=ALU.mult).then_inc(p.sev.sem, 1)))
            a1 = p.sac.inc()
            sch.add("scalar", lambda a, v3=v3: (
                a.wait_ge(p.sev.sem, v3),
                a.sign(fw[:], fsc[:]).then_inc(p.sac.sem, 1)))
            v4 = p.sev.inc()
            sch.add("vector", lambda v, a1=a1: (
                v.wait_ge(p.sac.sem, a1),
                v.tensor_tensor(fsc[:], fsc[:], fw[:], op=ALU.mult),
                v.drain(),
                v.tensor_scalar(fsc[:], fsc[:], 0.5, None, op0=ALU.is_gt),
                v.drain(),
                v.tensor_tensor(wq[:], fsc[:], fw[:],
                                op=ALU.mult).then_inc(p.sev.sem, 1)))
            sch.run()

        cB = ctx.enter_context(ExitStack())
        eB = cB.enter_context
        xn1 = eB(nc.sbuf_tensor(un("xn1"), [128, NT * C], F32))
        gmas = eB(nc.sbuf_tensor(un("gmas"), [128, C], F32))
        btas = eB(nc.sbuf_tensor(un("btas"), [128, C], F32))
        st2 = eB(nc.sbuf_tensor(un("st2"), [128, 8], F32))
        sq2 = eB(nc.sbuf_tensor(un("sq2"), [128, C], F32))
        tmp2 = eB(nc.sbuf_tensor(un("tmp2"), [128, C], F32))
        sch = Sched(nc)
        gb_t = p.sx.inc(32)
        sch.add("sync", lambda s: (
            s.dma_start(gmas[:], p.gma.ap()).then_inc(p.sx.sem, 16),
            s.dma_start(btas[:], p.bta.ap()).then_inc(p.sx.sem, 16)))
        sch.run()
        emit_ln(p, cB, r_in, None, keep_tok=xn1)
        sch = Sched(nc)
        for i in range(NT):
            s2 = st2
            xt = xn1[:, i * C:(i + 1) * C]
            v1 = p.sev.inc()
            sch.add("vector", lambda v, xt=xt, s2=s2: (
                v.drain(),
                v.reduce_sum(s2[:, 0:1], xt, axis=AX).then_inc(p.sev.sem, 1)))
            a1 = p.sac.inc()
            sch.add("scalar", lambda a, xt=xt, v1=v1: (
                a.wait_ge(p.sev.sem, v1),
                a.activation(sq2[:], xt, AF.Square).then_inc(p.sac.sem, 1)))
            v2 = p.sev.inc()
            sch.add("vector", lambda v, s2=s2, a1=a1: (
                v.wait_ge(p.sac.sem, a1),
                v.reduce_sum(s2[:, 1:2], sq2[:], axis=AX),
                v.drain(),
                v.tensor_scalar(s2[:, 2:3], s2[:, 0:1], 1.0 / C, None, op0=ALU.mult),
                v.drain(),
                v.tensor_tensor(s2[:, 3:4], s2[:, 2:3], s2[:, 2:3], op=ALU.mult),
                v.tensor_scalar(s2[:, 4:5], s2[:, 1:2], 1.0 / C, None, op0=ALU.mult),
                v.drain(),
                v.tensor_tensor(s2[:, 3:4], s2[:, 4:5], s2[:, 3:4], op=ALU.subtract),
                v.drain(),
                v.tensor_scalar(s2[:, 3:4], s2[:, 3:4], LN2E, None,
                                op0=ALU.add).then_inc(p.sev.sem, 1)))
            a2 = p.sac.inc()
            sch.add("scalar", lambda a, s2=s2, v2=v2: (
                a.wait_ge(p.sev.sem, v2),
                a.activation(s2[:, 4:5], s2[:, 3:4], AF.Sqrt).then_inc(p.sac.sem, 1)))
            v3 = p.sev.inc()
            sch.add("vector", lambda v, s2=s2, a2=a2: (
                v.wait_ge(p.sac.sem, a2),
                v.reciprocal(s2[:, 5:6], s2[:, 4:5]),
                v.drain(),
                v.scalar_tensor_tensor(s2[:, 6:7], s2[:, 2:3], -1.0, s2[:, 5:6],
                                       op0=ALU.mult, op1=ALU.mult).then_inc(p.sev.sem, 1)))
            a3 = p.sac.inc()
            sch.add("scalar", lambda a, xt=xt, s2=s2, v3=v3: (
                a.wait_ge(p.sev.sem, v3),
                a.wait_ge(p.sx.sem, gb_t),
                a.activation(tmp2[:], xt, AF.Identity, bias=s2[:, 6:7],
                             scale=s2[:, 5:6]).then_inc(p.sac.sem, 1)))
            v4 = p.sev.inc()
            sch.add("vector", lambda v, xt=xt, i=i, a3=a3: (
                v.wait_ge(p.sac.sem, a3),
                v.tensor_tensor(xt, tmp2[:], gmas[:], op=ALU.mult),
                v.drain(),
                v.tensor_tensor(xt, xt, btas[:], op=ALU.add),
                v.drain(),
                v.reduce_max(gcol[:, 1:2], xt, axis=AX, apply_absolute_value=True),
                v.drain(),
                (v.tensor_copy(gcol[:, 0:1], gcol[:, 1:2]) if i == 0 else
                 v.tensor_tensor(gcol[:, 0:1], gcol[:, 0:1], gcol[:, 1:2],
                                 op=ALU.max)).then_inc(p.sev.sem, 1)))
        last_ev = p.sev.n
        mt = p.smm.inc()
        sch.add("tensor", lambda t, le=last_ev: (
            t.wait_ge(p.sev.sem, le),
            t.transpose(ps_bc[0:1, 384:512], gcol[:, 0:1],
                        p.idf[:]).then_inc(p.smm.sem, 1)))
        v5 = p.sev.inc()
        sch.add("vector", lambda v, mt=mt: (
            v.memset(stg[0:1, 1:8], 0.0),
            v.wait_ge(p.smm.sem, mt),
            v.drain(),
            v.reduce_max(stg[0:1, 0:1], ps_bc[0:1, 384:512],
                         axis=AX).then_inc(p.sev.sem, 1)))
        ccd1 = p.scc.inc(16)
        ccg = p.sg.inc()
        sch.add("gpsimd", lambda g, v5=v5, ccd1=ccd1: (
            g.wait_ge(p.sev.sem, v5),
            g.dma_start(p.cc_in.ap(), stg[:]).then_inc(p.scc.sem, 16),
            g.wait_ge(p.scc.sem, ccd1),
            g.collective_compute(
                "AllReduce", ALU.max, replica_groups=[list(range(8))],
                ins=[p.cc_in.ap().opt()], outs=[p.cc_out.ap().opt()],
            ).then_inc(p.sg.sem, 1)))
        cc2 = p.scc.inc(16)
        sch.add("gpsimd", lambda g, ccg=ccg: (
            g.wait_ge(p.sg.sem, ccg),
            g.dma_start(sc[0:1, 2:3],
                        AP(p.cc_out, 0, [[8, 1], [1, 1]])).then_inc(p.scc.sem, 16)))
        v6 = p.sev.inc()
        sch.add("vector", lambda v, cc2=cc2: (
            v.wait_ge(p.scc.sem, cc2),
            v.reciprocal(sc[:, 5:6], sc[:, 2:3]),
            v.drain(),
            v.tensor_scalar(sc[:, 3:4], sc[:, 5:6], QB, None, op0=ALU.mult),
            v.tensor_tensor(sc[:, 4:5], sc[:, 2:3], sc[:, 0:1], op=ALU.mult),
            v.drain(),
            v.tensor_scalar(sc[:, 4:5], sc[:, 4:5], 1.0 / QB, None,
                            op0=ALU.mult).then_inc(p.sev.sem, 1)))
        m1 = p.smm.inc()
        sch.add("tensor", lambda t, v6=v6: (
            t.wait_ge(p.sev.sem, v6),
            t.matmul(ps_bc[:, 1:2], p.ones_r[:], sc[:, 3:4], start=True, stop=True),
            t.matmul(ps_bc[:, 2:3], p.ones_r[:], sc[:, 4:5], start=True,
                     stop=True).then_inc(p.smm.sem, 1)))
        v7 = p.sev.inc()
        sch.add("vector", lambda v, m1=m1: (
            v.wait_ge(p.smm.sem, m1),
            v.tensor_copy(bcs[:, 1:3], ps_bc[:, 1:3]).then_inc(p.sev.sem, 1)))
        sch.run()

        xqt = e(nc.sbuf_tensor(un("xqt"), [128, NT * C], BF16))
        xqF = e(nc.sbuf_tensor(un("xqF"), [128, 8 * T], BF16))
        rb2 = [e(nc.sbuf_tensor(un(f"f0rb{k}"), [128, C], F32)) for k in range(2)]
        rob = [e(nc.sbuf_tensor(un(f"f0ro{k}"), [128, C], F32)) for k in range(2)]
        sch = Sched(nc)
        v8 = p.sev.inc()
        sch.add("vector", lambda v: (
            v.drain(),
            v.tensor_scalar(xqt[:], xn1[:], bcs[:, 1:2], QB - EPS, op0=ALU.mult,
                            op1=ALU.min),
            v.drain(),
            v.tensor_scalar(xqt[:], xqt[:], -(QB - EPS), None,
                            op0=ALU.max).then_inc(p.sev.sem, 1)))
        trb_free = [0, 0]
        for i in range(NT):
            for cc in range(NCC):
                tb = (i * NCC + cc) % 2
                m1 = p.smm.inc()
                fr = trb_free[tb]
                sch.add("tensor", lambda t, i=i, cc=cc, tb=tb, fr=fr, v8=v8: (
                    t.wait_ge(p.sev.sem, v8),
                    t.wait_ge(p.sev.sem, fr),
                    t.transpose(pst[tb][:], xqt[:, i * C + cc * 128: i * C + (cc + 1) * 128],
                                p.identity[:]).then_inc(p.smm.sem, 1)))
                v9 = p.sev.inc()
                sch.add("vector", lambda v, i=i, cc=cc, tb=tb, m1=m1: (
                    v.wait_ge(p.smm.sem, m1),
                    v.tensor_copy(xqF[:, cc * T + i * 128: cc * T + (i + 1) * 128],
                                  pst[tb][:]).then_inc(p.sev.sem, 1)))
                trb_free[tb] = v9
        tr_done = p.sev.n
        ot_hist = [0, 0]
        for i in range(NT):
            bb = i % 2
            x_t = p.sxp[bb].inc(16)
            ev_prev = p.sev.n
            sch.add("sync", lambda s, i=i, bb=bb, ep=ev_prev, ot=ot_hist[bb]: (
                s.wait_ge(p.sev.sem, ep),
                s.wait_ge(p.sop[bb].sem, ot),
                s.dma_start(rb2[bb][:], rtile(r_in, i)).then_inc(p.sxp[bb].sem, 16)))
            for ch in range(2):
                b = (i * 2 + ch) % 2
                fa = p.sev.n
                m1 = p.smm.inc()
                def mmf(t, i=i, ch=ch, b=b, fa=fa, td=tr_done):
                    t.wait_ge(p.sev.sem, max(fa, td))
                    for cc in range(8):
                        ins = t.matmul(ps[b][:],
                                       xqF[:, cc * T + i * 128: cc * T + (i + 1) * 128],
                                       wq[:, cc * 1024 + ch * 512: cc * 1024 + ch * 512 + 512],
                                       start=(cc == 0), stop=(cc == 7))
                    ins.then_inc(p.smm.sem, 1)
                sch.add("tensor", mmf)
                p.sev.inc()
                sch.add("vector", lambda v, ch=ch, b=b, bb=bb, m1=m1, x_t=x_t,
                        ot=ot_hist[bb]: (
                    v.wait_ge(p.smm.sem, m1),
                    v.wait_ge(p.sxp[bb].sem, x_t),
                    v.wait_ge(p.sop[bb].sem, ot),
                    v.scalar_tensor_tensor(rob[bb][:, ch * 512:(ch + 1) * 512], ps[b][:],
                                           bcs[:, 2:3], rb2[bb][:, ch * 512:(ch + 1) * 512],
                                           op0=ALU.mult, op1=ALU.add).then_inc(p.sev.sem, 1)))
            o_t = p.sop[bb].inc(16)
            last_v = p.sev.n
            sch.add("sync", lambda s, i=i, bb=bb, lv=last_v: (
                s.wait_ge(p.sev.sem, lv),
                s.dma_start(rtile(r_out, i), rob[bb][:]).then_inc(p.sop[bb].sem, 16)))
            ot_hist[bb] = o_t
        sch.add("sync", lambda s, n0=p.sop[0].n, n1=p.sop[1].n: (
            s.wait_ge(p.sop[0].sem, n0), s.wait_ge(p.sop[1].sem, n1)))
        sch.run()


def emit_l1_attn(p, r_in, r_out):
    nc = p.nc
    with ExitStack() as ctx:
        e = ctx.enter_context
        qF = e(nc.sbuf_tensor(un("qF1"), [128, 8 * T], BF16))
        kF = e(nc.sbuf_tensor(un("kF1"), [128, 8 * T], BF16))
        vtok = e(nc.sbuf_tensor(un("vt1"), [128, NT * 1024], BF16))
        PT = [e(nc.sbuf_tensor(un(f"PT1{k}"), [128, 512], BF16)) for k in range(4)]
        mg = e(nc.sbuf_tensor(un("mg1"), [128, 40], F32))
        negM = e(nc.sbuf_tensor(un("negM1"), [1, 8], F32))
        negMbc = e(nc.sbuf_tensor(un("negMbc1"), [128, 8], F32))
        lrow = e(nc.sbuf_tensor(un("lrow1"), [1, 512], F32))
        linv = e(nc.sbuf_tensor(un("linv1"), [1, 512], F32))
        lbc = e(nc.sbuf_tensor(un("lbc1"), [128, 512], F32))

        cproj = ctx.enter_context(ExitStack())
        ep = cproj.enter_context
        ps = [ep(nc.psum_tensor(un(f"a1ps{k}"), [128, 512], F32)) for k in range(4)]
        xnF = ep(nc.sbuf_tensor(un("xnF1"), [128, 8 * T], BF16))
        wbuf = ep(nc.sbuf_tensor(un("wbuf1"), [128, 8 * 1024], BF16))
        ctab = ep(nc.sbuf_tensor(un("ctab"), [128, T], F32))
        stab = ep(nc.sbuf_tensor(un("stab"), [128, T], F32))
        rsc = [ep(nc.sbuf_tensor(un(f"rsc{k}"), [128, 512], F32)) for k in range(4)]
        emit_ln(p, cproj, r_in, xnF)

        for wsel, tab_c, tab_s, dstF in ((p.wqT1, p.cq1, p.sq1, qF),
                                         (p.wkT1, p.ck1, p.sk1, kF)):
            sch = Sched(nc)
            prev_mm = p.smm.n
            prev_ev = p.sev.n
            w_t = p.sw.inc(16)
            sch.add("gpsimd", lambda g, wsel=wsel, pm=prev_mm, w_t=w_t: (
                g.wait_ge(p.smm.sem, pm),
                g.dma_start(wbuf[:], hbm_tiled_ap(wsel, 1024, 1024)).then_inc(p.sw.sem, 16)))
            t_t = p.sx.inc(32)
            sch.add("sync", lambda s, tab_c=tab_c, tab_s=tab_s, pe=prev_ev: (
                s.wait_ge(p.sev.sem, pe),
                s.dma_start(ctab[:], tab_c.ap()).then_inc(p.sx.sem, 16),
                s.dma_start(stab[:], tab_s.ap()).then_inc(p.sx.sem, 16)))
            for tt in range(4):
                for oc in range(4):
                    b0 = (tt * 4 + oc) % 2 * 2
                    fa = p.sev.n
                    m1 = p.smm.inc()
                    def mmf(t, oc=oc, tt=tt, b0=b0, fa=fa, w_t=w_t):
                        t.wait_ge(p.sw.sem, w_t)
                        t.wait_ge(p.sev.sem, fa)
                        for half in range(2):
                            ocg = oc + 4 * half
                            for cc in range(8):
                                ins = t.matmul(
                                    ps[b0 + half][:],
                                    wbuf[:, cc * 1024 + ocg * 128: cc * 1024 + (ocg + 1) * 128],
                                    xnF[:, cc * T + tt * 512: cc * T + tt * 512 + 512],
                                    start=(cc == 0), stop=(cc == 7))
                        ins.then_inc(p.smm.sem, 1)
                    sch.add("tensor", mmf)
                    p.sev.inc()
                    def evf(v, oc=oc, tt=tt, b0=b0, m1=m1, t_t=t_t, dstF=dstF):
                        v.wait_ge(p.smm.sem, m1)
                        v.wait_ge(p.sx.sem, t_t)
                        v.drain()
                        ts = slice(tt * 512, tt * 512 + 512)
                        v.tensor_tensor(rsc[0][:], ps[b0][:], ctab[:, ts], op=ALU.mult)
                        v.tensor_tensor(rsc[1][:], ps[b0 + 1][:], stab[:, ts], op=ALU.mult)
                        v.tensor_tensor(rsc[2][:], ps[b0][:], stab[:, ts], op=ALU.mult)
                        v.tensor_tensor(rsc[3][:], ps[b0 + 1][:], ctab[:, ts], op=ALU.mult)
                        v.drain()
                        v.tensor_tensor(dstF[:, oc * T + tt * 512: oc * T + tt * 512 + 512],
                                        rsc[0][:], rsc[1][:], op=ALU.subtract)
                        v.tensor_tensor(dstF[:, (4 + oc) * T + tt * 512: (4 + oc) * T + tt * 512 + 512],
                                        rsc[2][:], rsc[3][:],
                                        op=ALU.add).then_inc(p.sev.sem, 1)
                    sch.add("vector", evf)
            sch.run()
        sch = Sched(nc)
        prev_mm = p.smm.n
        w_t = p.sw.inc(16)
        sch.add("gpsimd", lambda g, pm=prev_mm, w_t=w_t: (
            g.wait_ge(p.smm.sem, pm),
            g.dma_start(wbuf[:], hbm_tiled_ap(p.wvT1, 1024, 1024)).then_inc(p.sw.sem, 16)))
        for i in range(NT):
            for oh in range(2):
                b = (i * 2 + oh) % 2
                fa = p.sev.n
                m1 = p.smm.inc()
                def mmf(t, i=i, oh=oh, b=b, fa=fa, w_t=w_t):
                    t.wait_ge(p.sw.sem, w_t)
                    t.wait_ge(p.sev.sem, fa)
                    for cc in range(8):
                        ins = t.matmul(ps[b][:],
                                       xnF[:, cc * T + i * 128: cc * T + (i + 1) * 128],
                                       wbuf[:, cc * 1024 + oh * 512: cc * 1024 + oh * 512 + 512],
                                       start=(cc == 0), stop=(cc == 7))
                    ins.then_inc(p.smm.sem, 1)
                sch.add("tensor", mmf)
                p.sev.inc()
                sch.add("vector", lambda v, i=i, oh=oh, b=b, m1=m1: (
                    v.wait_ge(p.smm.sem, m1),
                    v.tensor_copy(vtok[:, i * 1024 + oh * 512: i * 1024 + oh * 512 + 512],
                                  ps[b][:]).then_inc(p.sev.sem, 1)))
        sch.run()
        cproj.close()

        ps = [e(nc.psum_tensor(un(f"a1pt{k}"), [128, 512], F32)) for k in range(2)]
        ps_o = [e(nc.psum_tensor(un(f"a1po{k}"), [128, 512], F32)) for k in range(2)]
        ps_l = [e(nc.psum_tensor(un(f"a1pl{k}"), [128, 512], F32)) for k in range(2)]
        ps_bc = e(nc.psum_tensor(un("a1pb"), [128, 512], F32))
        OF = e(nc.sbuf_tensor(un("OF1"), [128, 8 * T], BF16))
        wos = e(nc.sbuf_tensor(un("wos1"), [128, 8 * 1024], BF16))
        rb2 = [e(nc.sbuf_tensor(un(f"a1rb{k}"), [128, C], F32)) for k in range(2)]
        rob = [e(nc.sbuf_tensor(un(f"a1ro{k}"), [128, C], F32)) for k in range(2)]
        sch = Sched(nc)
        w_t = p.sw.inc(16)
        sch.add("gpsimd", lambda g, w_t=w_t: (
            g.dma_start(wos[:], hbm_tiled_ap(p.woT1, 1024, 1024)).then_inc(p.sw.sem, 16)))
        sch.run()

        act_prev = [0]
        for hp in range(8):
            def mk_head(h):
                cc = h // 4
                ro = 32 * (h % 4)
                ro2 = 64 * (h % 2)
                jc = h // 2
                def mmA(t, b, qi, kc, n, last, cc=cc, ro=ro):
                    t.matmul(ps[b][:, 0:n],
                             qF[ro:ro + 32, cc * T + qi * 128: cc * T + qi * 128 + 128],
                             kF[ro:ro + 32, cc * T + kc: cc * T + kc + n],
                             start=True, stop=False, tile_position=(ro, 0))
                    ins = t.matmul(ps[b][:, 0:n],
                             qF[ro:ro + 32, (4 + cc) * T + qi * 128: (4 + cc) * T + qi * 128 + 128],
                             kF[ro:ro + 32, (4 + cc) * T + kc: (4 + cc) * T + kc + n],
                             start=False, stop=True,
                             tile_position=(ro, 0))
                    if last:
                        ins.then_inc(p.smm.sem, 1)
                def mmB(t, b, g, kb, last, cc=cc, ro=ro):
                    t.matmul(ps[b][:],
                             kF[ro:ro + 32, cc * T + kb * 128: cc * T + (kb + 1) * 128],
                             qF[ro:ro + 32, cc * T + g * 512: cc * T + g * 512 + 512],
                             start=True, stop=False, tile_position=(ro, 0))
                    ins = t.matmul(ps[b][:],
                             kF[ro:ro + 32, (4 + cc) * T + kb * 128: (4 + cc) * T + (kb + 1) * 128],
                             qF[ro:ro + 32, (4 + cc) * T + g * 512: (4 + cc) * T + g * 512 + 512],
                             start=False, stop=True,
                             tile_position=(ro, 0))
                    if last:
                        ins.then_inc(p.smm.sem, 1)
                def pv_lhs(kb, h=h):
                    return vtok[:, kb * 1024 + h * 64: kb * 1024 + h * 64 + 64]
                def of_dst(g, ro2=ro2, jc=jc):
                    return OF[ro2:ro2 + 64, jc * T + g * 512: jc * T + g * 512 + 512]
                return dict(mmA=mmA, mmB=mmB, pv_lhs=pv_lhs, of_dst=of_dst)
            attn_core(p, nc, ps, ps_o, ps_l, ps_bc, PT, mg, negM, negMbc,
                      lrow, linv, lbc, [mk_head(2 * hp), mk_head(2 * hp + 1)],
                      act_prev[0])
            act_prev[0] = p.sac.n

        sch = Sched(nc)
        ot_hist = [0, 0]
        for i in range(NT):
            bb = i % 2
            x_t = p.sxp[bb].inc(16)
            ev_prev = p.sev.n
            sch.add("sync", lambda s, i=i, bb=bb, ep=ev_prev, ot=ot_hist[bb]: (
                s.wait_ge(p.sev.sem, ep),
                s.wait_ge(p.sop[bb].sem, ot),
                s.dma_start(rb2[bb][:], rtile(r_in, i)).then_inc(p.sxp[bb].sem, 16)))
            for ch in range(2):
                b = (i * 2 + ch) % 2
                fa = p.sev.n
                m1 = p.smm.inc()
                def mmf(t, i=i, ch=ch, b=b, fa=fa, w_t=w_t):
                    t.wait_ge(p.sw.sem, w_t)
                    t.wait_ge(p.sev.sem, fa)
                    for oc in range(8):
                        ins = t.matmul(ps[b][:],
                                       OF[:, oc * T + i * 128: oc * T + i * 128 + 128],
                                       wos[:, oc * 1024 + ch * 512: oc * 1024 + ch * 512 + 512],
                                       start=(oc == 0), stop=(oc == 7))
                    ins.then_inc(p.smm.sem, 1)
                sch.add("tensor", mmf)
                p.sev.inc()
                sch.add("vector", lambda v, ch=ch, b=b, bb=bb, m1=m1, x_t=x_t,
                        ot=ot_hist[bb]: (
                    v.wait_ge(p.smm.sem, m1),
                    v.wait_ge(p.sxp[bb].sem, x_t),
                    v.wait_ge(p.sop[bb].sem, ot),
                    v.tensor_tensor(rob[bb][:, ch * 512:(ch + 1) * 512], ps[b][:],
                                    rb2[bb][:, ch * 512:(ch + 1) * 512],
                                    op=ALU.add).then_inc(p.sev.sem, 1)))
            o_t = p.sop[bb].inc(16)
            last_v = p.sev.n
            sch.add("sync", lambda s, i=i, bb=bb, lv=last_v: (
                s.wait_ge(p.sev.sem, lv),
                s.dma_start(rtile(r_out, i), rob[bb][:]).then_inc(p.sop[bb].sem, 16)))
            ot_hist[bb] = o_t
        sch.add("sync", lambda s, n0=p.sop[0].n, n1=p.sop[1].n: (
            s.wait_ge(p.sop[0].sem, n0), s.wait_ge(p.sop[1].sem, n1)))
        sch.run()


def emit_l1_ffn(p, r_in, r_out):
    nc = p.nc
    BIG = 1e30
    with ExitStack() as ctx:
        e = ctx.enter_context
        xnF = e(nc.sbuf_tensor(un("xnF2"), [128, 8 * T], BF16))
        lgt = e(nc.sbuf_tensor(un("lgt"), [128, NT * 32], F32))
        gbs = e(nc.sbuf_tensor(un("gbs"), [128, 32], F32))
        cmb = e(nc.sbuf_tensor(un("cmb"), [128, NT * 32], F32))
        cmbT4 = e(nc.sbuf_tensor(un("cmbT4"), [128, T], F32))
        idf = e(nc.sbuf_tensor(un("idf"), [128, 128], F32))

        sch = Sched(nc)
        p.sev.inc()
        sch.add("vector", lambda v: (
            v.tensor_copy(idf[:], p.identity[:]).then_inc(p.sev.sem, 1)))
        p.take_drip(sch, len(p.drip))
        sch.add("sync", lambda s, wj=p.wc_join: s.wait_ge(p.swp[0].sem, wj))
        sch.run()

        with ExitStack() as cB:
            eB = cB.enter_context
            xn1 = eB(nc.sbuf_tensor(un("xn1g"), [128, NT * C], F32))
            gws = eB(nc.sbuf_tensor(un("gws"), [128, 8 * 32], F32))
            scrt = eB(nc.sbuf_tensor(un("scrt"), [128, 32], F32))
            scb = eB(nc.sbuf_tensor(un("scb"), [128, 32], F32))
            grp = eB(nc.sbuf_tensor(un("grp"), [128, 16], F32))
            msk = eB(nc.sbuf_tensor(un("msk"), [128, 32], F32))
            mw = eB(nc.sbuf_tensor(un("mw"), [128, 32], F32))
            xT32 = eB(nc.sbuf_tensor(un("xT32"), [128, C], F32))
            pst = [eB(nc.psum_tensor(un(f"gpt{k}"), [128, 128], F32)) for k in range(2)]
            ps_g = eB(nc.psum_tensor(un("psg"), [128, 512], F32))
            scrt = eB(nc.sbuf_tensor(un("scrt"), [128, 32], F32))
            scb = eB(nc.sbuf_tensor(un("scb"), [128, 32], F32))
            grp = eB(nc.sbuf_tensor(un("grp"), [128, 16], F32))
            msk = eB(nc.sbuf_tensor(un("msk"), [128, 32], F32))
            mw = eB(nc.sbuf_tensor(un("mw"), [128, 32], F32))
            sch = Sched(nc)
            g_t = p.sx.inc(32)
            sch.add("sync", lambda s: (
                s.dma_start(gws[:], hbm_tiled_ap(p.gwT, 1024, 32)).then_inc(p.sx.sem, 16),
                s.dma_start(gbs[:], p.gbias.ap()).then_inc(p.sx.sem, 16)))
            sch.run()
            emit_ln(p, cB, r_in, xnF, keep_tok=xn1)
            def mk_route(i, a1):
                def route(v, i=i, a1=a1):
                        v.wait_ge(p.sac.sem, a1)
                        v.drain()
                        v.tensor_tensor(scb[:], scrt[:], gbs[:], op=ALU.add)
                        v.drain()
                        def g4(off):
                            return AP(scb, off, [[32, 128], [4, 8]])
                        v.tensor_tensor(grp[:, 0:1].tensor[:, 0:8] if False else
                                        AP(grp, 0, [[16, 128], [1, 8]]),
                                        g4(0), g4(1), op=ALU.max)
                        v.tensor_tensor(AP(grp, 8 * 0 + 0, [[16, 128], [1, 8]]),
                                        g4(0), g4(1), op=ALU.max) if False else None
                        v.drain()
                        v.tensor_tensor(AP(msk, 0, [[32, 128], [1, 8]]), g4(0), g4(1),
                                        op=ALU.min)
                        v.tensor_tensor(AP(msk, 8, [[32, 128], [1, 8]]), g4(2), g4(3),
                                        op=ALU.min)
                        v.tensor_tensor(AP(msk, 16, [[32, 128], [1, 8]]), g4(2), g4(3),
                                        op=ALU.max)
                        v.drain()
                        v.tensor_tensor(AP(grp, 8, [[16, 128], [1, 8]]),
                                        AP(grp, 0, [[16, 128], [1, 8]]),
                                        AP(msk, 16, [[32, 128], [1, 8]]), op=ALU.max)
                        v.tensor_tensor(AP(mw, 0, [[32, 128], [1, 8]]),
                                        AP(grp, 0, [[16, 128], [1, 8]]),
                                        AP(msk, 16, [[32, 128], [1, 8]]), op=ALU.min)
                        v.tensor_tensor(AP(mw, 8, [[32, 128], [1, 8]]),
                                        AP(msk, 0, [[32, 128], [1, 8]]),
                                        AP(msk, 8, [[32, 128], [1, 8]]), op=ALU.max)
                        v.drain()
                        v.tensor_tensor(AP(mw, 16, [[32, 128], [1, 8]]),
                                        AP(mw, 0, [[32, 128], [1, 8]]),
                                        AP(mw, 8, [[32, 128], [1, 8]]), op=ALU.max)
                        v.drain()
                        v.tensor_tensor(AP(grp, 0, [[16, 128], [1, 8]]),
                                        AP(grp, 8, [[16, 128], [1, 8]]),
                                        AP(mw, 16, [[32, 128], [1, 8]]), op=ALU.add)
                        v.drain()
                        v.tensor_copy(AP(mw, 24, [[32, 128], [1, 8]]),
                                      AP(grp, 0, [[16, 128], [1, 8]]))
                        for _ in range(3):
                            v.drain()
                            v.reduce_max(grp[:, 8:9], AP(mw, 24, [[32, 128], [1, 8]]),
                                         axis=AX)
                            v.drain()
                            v.tensor_scalar(AP(msk, 0, [[32, 128], [1, 8]]),
                                            AP(mw, 24, [[32, 128], [1, 8]]),
                                            grp[:, 8:9], None, op0=ALU.is_ge)
                            v.drain()
                            v.scalar_tensor_tensor(AP(mw, 24, [[32, 128], [1, 8]]),
                                                   AP(msk, 0, [[32, 128], [1, 8]]),
                                                   -BIG, AP(mw, 24, [[32, 128], [1, 8]]),
                                                   op0=ALU.mult, op1=ALU.add)
                        v.drain()
                        v.reduce_max(grp[:, 8:9], AP(mw, 24, [[32, 128], [1, 8]]), axis=AX)
                        v.drain()
                        v.tensor_scalar(AP(msk, 0, [[32, 128], [1, 8]]),
                                        AP(grp, 0, [[16, 128], [1, 8]]),
                                        grp[:, 8:9], None, op0=ALU.is_ge)
                        v.drain()
                        v.tensor_copy(AP(mw, 0, [[32, 128], [4, 8], [1, 4]]),
                                      AP(msk, 0, [[32, 128], [1, 8], [0, 4]]))
                        v.drain()
                        v.tensor_tensor(msk[:], scb[:], mw[:], op=ALU.mult)
                        v.drain()
                        v.tensor_copy(mw[:], msk[:])
                        for _ in range(7):
                            v.drain()
                            v.reduce_max(grp[:, 8:9], mw[:], axis=AX)
                            v.drain()
                            v.tensor_scalar(scb[:], mw[:], grp[:, 8:9], None, op0=ALU.is_ge)
                            v.drain()
                            v.scalar_tensor_tensor(mw[:], scb[:], -BIG, mw[:],
                                                   op0=ALU.mult, op1=ALU.add)
                        v.drain()
                        v.reduce_max(grp[:, 8:9], mw[:], axis=AX)
                        v.drain()
                        v.tensor_scalar(scb[:], msk[:], grp[:, 8:9], None, op0=ALU.is_ge)
                        v.drain()
                        v.tensor_tensor(msk[:], scrt[:], scb[:], op=ALU.mult)
                        v.drain()
                        v.reduce_sum(grp[:, 9:10], msk[:], axis=AX)
                        v.drain()
                        v.tensor_scalar(grp[:, 9:10], grp[:, 9:10], 1e-20, None, op0=ALU.add)
                        v.drain()
                        v.reciprocal(grp[:, 10:11], grp[:, 9:10])
                        v.drain()
                        v.tensor_scalar(cmb[:, i * 32:(i + 1) * 32], msk[:],
                                        grp[:, 10:11], 2.5, op0=ALU.mult,
                                        op1=ALU.mult).then_inc(p.sev.sem, 1)
                    
                return route
            sch = Sched(nc)
            for i in range(NT):
                for cc in range(NCC):
                    tb = cc % 2
                    m1 = p.smm.inc()
                    fr = p.sev.n
                    sch.add("tensor", lambda t, i=i, cc=cc, tb=tb, fr=fr: (
                        t.wait_ge(p.sev.sem, fr),
                        t.transpose(pst[tb][:], xn1[:, i * C + cc * 128: i * C + (cc + 1) * 128],
                                    idf[:]).then_inc(p.smm.sem, 1)))
                    p.sev.inc()
                    sch.add("vector", lambda v, cc=cc, tb=tb, m1=m1: (
                        v.wait_ge(p.smm.sem, m1),
                        v.tensor_copy(xT32[:, cc * 128:(cc + 1) * 128],
                                      pst[tb][:]).then_inc(p.sev.sem, 1)))
                tr_ev = p.sev.n
                m2 = p.smm.inc()
                sch.add("tensor", lambda t, i=i, te=tr_ev, g_t=g_t: (
                    t.wait_ge(p.sev.sem, te),
                    t.wait_ge(p.sx.sem, g_t),
                    [t.matmul(ps_g[:, 0:32], xT32[:, cc * 128:(cc + 1) * 128],
                              gws[:, cc * 32:(cc + 1) * 32],
                              start=(cc == 0), stop=(cc == 7)) for cc in range(8)][-1]
                    .then_inc(p.smm.sem, 1)))
                vg = p.sev.inc()
                sch.add("vector", lambda v, i=i, m2=m2: (
                    v.wait_ge(p.smm.sem, m2),
                    v.tensor_copy(lgt[:, i * 32:(i + 1) * 32],
                                  ps_g[:, 0:32]).then_inc(p.sev.sem, 1)))
                a1 = p.sac.inc()
                sch.add("scalar", lambda a, i=i, vg=vg: (
                    a.wait_ge(p.sev.sem, vg),
                    a.activation(scrt[:], lgt[:, i * 32:(i + 1) * 32],
                                 AF.Sigmoid).then_inc(p.sac.sem, 1)))
                p.sev.inc()
                sch.add("vector", mk_route(i, a1))
                rt_ev = p.sev.n
                m3 = p.smm.inc()
                sch.add("tensor", lambda t, i=i, re=rt_ev: (
                    t.wait_ge(p.sev.sem, re),
                    t.transpose(pst[0][0:32, 0:128], cmb[:, i * 32:(i + 1) * 32],
                                p.idf[:]).then_inc(p.smm.sem, 1)))
                p.sev.inc()
                sch.add("vector", lambda v, i=i, m3=m3: (
                    v.wait_ge(p.smm.sem, m3),
                    v.tensor_copy(cmbT4[0:32, i * 128:(i + 1) * 128],
                                  pst[0][0:32, 0:128]).then_inc(p.sev.sem, 1)))
            rep_ev = p.sev.n
            p.rep_t = p.sx.inc(48)
            sch.add("sync", lambda s, re=rep_ev, rt=p.rep_t: (
                s.wait_ge(p.sev.sem, re),
                s.dma_start(cmbT4[32:64, :], cmbT4[0:32, :]).then_inc(p.sx.sem, 16),
                s.dma_start(cmbT4[64:96, :], cmbT4[0:32, :]).then_inc(p.sx.sem, 16),
                s.dma_start(cmbT4[96:128, :], cmbT4[0:32, :]).then_inc(p.sx.sem, 16)))
            sch.run()
        with ExitStack() as cC:
            eC = cC.enter_context
            wg = [eC(nc.sbuf_tensor(un(f"wgb{k}"), [128, 8 * 512], BF16)) for k in range(2)]
            wu = [eC(nc.sbuf_tensor(un(f"wub{k}"), [128, 8 * 512], BF16)) for k in range(2)]
            wd = [eC(nc.sbuf_tensor(un(f"wdb{k}"), [128, 4 * 1024], BF16)) for k in range(2)]
            Gs = [eC(nc.sbuf_tensor(un(f"Gs{k}"), [128, 4 * 256], BF16)) for k in range(2)]
            Us = [eC(nc.sbuf_tensor(un(f"Us{k}"), [128, 4 * 256], BF16)) for k in range(2)]
            Ps = [eC(nc.sbuf_tensor(un(f"Ps{k}"), [128, 4 * 256], BF16)) for k in range(2)]
            cbc = [eC(nc.sbuf_tensor(un(f"cbc{k}"), [128, 256], F32)) for k in range(2)]
            routed = eC(nc.sbuf_tensor(un("routed"), [128, NT * C], F32))
            psb = [eC(nc.psum_tensor(un(f"psb{k}"), [128, 512], F32)) for k in range(4)]
            dn = [eC(nc.psum_tensor(un(f"dn{k}"), [128, 512], F32)) for k in range(4)]
            last_use = {}
            pend = [None]
            rt_in = p.sw.inc(16 * NT)

            def preres(g):
                for i in range(NT):
                    g.dma_start(routed[:, i * C:(i + 1) * C],
                                rtile(r_in, i)).then_inc(p.sw.sem, 16)

            def emit_down(sch, pd):
                (ee, tcb, par, v1) = pd
                tp = tcb % 2
                pdpar = par
                for q in range(4):
                    tsub, hh = q // 2, q % 2
                    bq = q
                    dfree = p.sev.n
                    m2 = p.smm.inc()
                    def dmf(t, par=par, tsub=tsub, hh=hh, bq=bq, v1=v1, df=dfree,
                            tp=tp):
                        t.wait_ge(p.sev.sem, max(v1, df))
                        for ic in range(4):
                            ins = t.matmul(
                                dn[bq][:],
                                Ps[tp][:, ic * 256 + tsub * 128: ic * 256 + tsub * 128 + 128],
                                wd[par][:, ic * 1024 + hh * 512: ic * 1024 + hh * 512 + 512],
                                start=(ic == 0), stop=(ic == 3))
                        ins.then_inc(p.smm.sem, 1)
                    sch.add("tensor", dmf)
                    p.sev.inc()
                    def evd(v, ee=ee, tcb=tcb, tsub=tsub, hh=hh, bq=bq, m2=m2):
                        v.wait_ge(p.smm.sem, m2)
                        if ee == 0:
                            v.wait_ge(p.sw.sem, rt_in)
                        dst = routed[:, (tcb * 2 + tsub) * C + hh * 512:
                                     (tcb * 2 + tsub) * C + hh * 512 + 512]
                        v.tensor_tensor(dst, dst, dn[bq][:],
                                        op=ALU.add).then_inc(p.sev.sem, 1)
                    sch.add("vector", evd)
                last_use[pdpar] = p.smm.n

            def emit_store(sch, tcb_done):
                fin = p.sev.n
                def stf(s, fin=fin, tcb_done=tcb_done):
                    s.wait_ge(p.sev.sem, fin)
                    for tsub in range(2):
                        i = tcb_done * 2 + tsub
                        s.dma_start(rtile(p.r_b, i), routed[:, i * C:(i + 1) * C]
                                    ).then_inc(p.sop[i % 2].sem, 16)
                sch.add("sync", stf)
                for tsub in range(2):
                    p.sop[(tcb_done * 2 + tsub) % 2].inc(16)

            for ee in range(32):
                par = ee % 2
                sch = Sched(nc)
                if ee == 0:
                    sch.add("gpsimd", preres)
                pu = last_use.get(par, 0)
                w_t = p.sfp[par].inc(48)
                def ldf(s, ee=ee, par=par, pu=pu):
                    s.wait_ge(p.smm.sem, pu)
                    base = ee * 512 * 1024
                    s.dma_start(wg[par][:], AP(p.wc_g, base,
                                               [[512, 128], [128 * 512, 8], [1, 512]])
                                ).then_inc(p.sfp[par].sem, 16)
                    s.dma_start(wu[par][:], AP(p.wc_u, base,
                                               [[512, 128], [128 * 512, 8], [1, 512]])
                                ).then_inc(p.sfp[par].sem, 16)
                    s.dma_start(wd[par][:], AP(p.wc_d, base,
                                               [[1024, 128], [128 * 1024, 4], [1, 1024]])
                                ).then_inc(p.sfp[par].sem, 16)
                sch.add("sync", ldf)
                for tcb in range(8):
                    pi = ee * 8 + tcb
                    gp_ = (pi % 2) * 2
                    up_ = ((pi + 1) % 2) * 2
                    gb = (gp_, gp_ + 1)
                    ub = (up_, up_ + 1)
                    tp = tcb % 2
                    gu_free = (p.sac.n, p.sev.n)
                    v0 = p.sev.inc()
                    sch.add("vector", lambda v, ee=ee, tcb=tcb, tp=tp, rt=p.rep_t: (
                        v.wait_ge(p.sx.sem, rt),
                        v.drain(),
                        v.stream_shuffle(cbc[tp][:], cmbT4[:, tcb * 256:(tcb + 1) * 256],
                                         [ee] * 32).then_inc(p.sev.sem, 1)))
                    m1 = p.smm.inc()
                    def gmf(t, tcb=tcb, par=par, w_t=w_t, gf=gu_free, gb=gb, ub=ub):
                        t.wait_ge(p.sfp[par].sem, w_t)
                        t.wait_ge(p.sac.sem, gf[0])
                        for ic in range(4):
                            bank = ub[ic // 2]
                            col = (ic % 2) * 256
                            for cc in range(8):
                                ins = t.matmul(
                                    psb[bank][:, col:col + 256],
                                    wu[par][:, cc * 512 + ic * 128: cc * 512 + (ic + 1) * 128],
                                    xnF[:, cc * T + tcb * 256: cc * T + (tcb + 1) * 256],
                                    start=(cc == 0), stop=(cc == 7))
                        t.wait_ge(p.sev.sem, gf[1])
                        for ic in range(4):
                            bank = gb[ic // 2]
                            col = (ic % 2) * 256
                            for cc in range(8):
                                ins = t.matmul(
                                    psb[bank][:, col:col + 256],
                                    wg[par][:, cc * 512 + ic * 128: cc * 512 + (ic + 1) * 128],
                                    xnF[:, cc * T + tcb * 256: cc * T + (tcb + 1) * 256],
                                    start=(cc == 0), stop=(cc == 7))
                        ins.then_inc(p.smm.sem, 1)
                    sch.add("tensor", gmf)
                    if pend[0] is not None:
                        emit_down(sch, pend[0])
                        if pend[0][0] == 31:
                            emit_store(sch, pend[0][1])
                    a1 = p.sac.inc()
                    def sif(a, m1=m1, gb=gb, tp=tp):
                        a.wait_ge(p.smm.sem, m1)
                        for ic in range(4):
                            bank = gb[ic // 2]
                            col = (ic % 2) * 256
                            ins = a.activation(Gs[tp][:, ic * 256:(ic + 1) * 256],
                                               psb[bank][:, col:col + 256], AF.Silu)
                        ins.then_inc(p.sac.sem, 1)
                    sch.add("scalar", sif)
                    v1 = p.sev.inc()
                    def upf(v, m1=m1, a1=a1, v0=v0, ub=ub, tp=tp):
                        v.wait_ge(p.smm.sem, m1)
                        v.wait_ge(p.sac.sem, a1)
                        v.wait_ge(p.sev.sem, v0)
                        v.drain()
                        for ic in range(4):
                            bank = ub[ic // 2]
                            col = (ic % 2) * 256
                            v.tensor_tensor(Us[tp][:, ic * 256:(ic + 1) * 256],
                                            psb[bank][:, col:col + 256], cbc[tp][:],
                                            op=ALU.mult)
                        v.drain()
                        for ic in range(4):
                            ins = v.tensor_tensor(Ps[tp][:, ic * 256:(ic + 1) * 256],
                                                  Gs[tp][:, ic * 256:(ic + 1) * 256],
                                                  Us[tp][:, ic * 256:(ic + 1) * 256],
                                                  op=ALU.mult)
                        ins.then_inc(p.sev.sem, 1)
                    sch.add("vector", upf)
                    pend[0] = (ee, tcb, par, v1)
                    last_use[par] = p.smm.n
                sch.run()
            sch = Sched(nc)
            emit_down(sch, pend[0])
            emit_store(sch, 7)
            sch.add("sync", lambda s, n0=p.sop[0].n, n1=p.sop[1].n: (
                s.wait_ge(p.sop[0].sem, n0), s.wait_ge(p.sop[1].sem, n1)))
            sch.run()

        with ExitStack() as cD:
            eD = cD.enter_context
            shg = eD(nc.sbuf_tensor(un("shg"), [128, 8 * 1024], BF16))
            shu = eD(nc.sbuf_tensor(un("shu"), [128, 8 * 1024], BF16))
            shd = eD(nc.sbuf_tensor(un("shd"), [128, 8 * 1024], BF16))
            PsS = eD(nc.sbuf_tensor(un("PsS"), [128, 8 * T], BF16))
            gsc = eD(nc.sbuf_tensor(un("gsc2"), [128, 512], BF16))
            rb2 = [eD(nc.sbuf_tensor(un(f"shrb{k}"), [128, C], F32)) for k in range(2)]
            rob = [eD(nc.sbuf_tensor(un(f"shro{k}"), [128, C], F32)) for k in range(2)]
            psA = [eD(nc.psum_tensor(un(f"psA{k}"), [128, 512], F32)) for k in range(2)]
            psB = [eD(nc.psum_tensor(un(f"psB{k}"), [128, 512], F32)) for k in range(2)]
            sch = Sched(nc)
            w_t0 = p.sw.inc(32)
            w_t = p.scc.inc(16)
            tflat = [[8192, 128], [1, 8192]]
            sch.add("gpsimd", lambda g: (
                g.dma_start(shg[:], AP(p.shgT, 0, tflat)).then_inc(p.sw.sem, 16),
                g.dma_start(shu[:], AP(p.shuT, 0, tflat)).then_inc(p.sw.sem, 16),
                g.dma_start(shd[:], AP(p.shdT, 0, tflat)).then_inc(p.scc.sem, 16)))
            for i2c in range(8):
                for tt in range(4):
                    b = (i2c * 4 + tt) % 2
                    fa = p.sev.n
                    m1 = p.smm.inc()
                    def gmf(t, i2c=i2c, tt=tt, b=b, fa=fa, w_t=w_t0):
                        t.wait_ge(p.sw.sem, w_t)
                        t.wait_ge(p.sev.sem, fa)
                        for cc in range(8):
                            t.matmul(psA[b][:],
                                     shg[:, cc * 1024 + i2c * 128: cc * 1024 + (i2c + 1) * 128],
                                     xnF[:, cc * T + tt * 512: cc * T + tt * 512 + 512],
                                     start=(cc == 0), stop=(cc == 7))
                        for cc in range(8):
                            ins = t.matmul(psB[b][:],
                                     shu[:, cc * 1024 + i2c * 128: cc * 1024 + (i2c + 1) * 128],
                                     xnF[:, cc * T + tt * 512: cc * T + tt * 512 + 512],
                                     start=(cc == 0), stop=(cc == 7))
                        ins.then_inc(p.smm.sem, 1)
                    sch.add("tensor", gmf)
                    a1 = p.sac.inc()
                    sch.add("scalar", lambda a, b=b, m1=m1: (
                        a.wait_ge(p.smm.sem, m1),
                        a.activation(gsc[:], psA[b][:], AF.Silu).then_inc(p.sac.sem, 1)))
                    p.sev.inc()
                    sch.add("vector", lambda v, i2c=i2c, tt=tt, b=b, a1=a1: (
                        v.wait_ge(p.sac.sem, a1),
                        v.tensor_tensor(PsS[:, i2c * T + tt * 512: i2c * T + tt * 512 + 512],
                                        psB[b][:], gsc[:],
                                        op=ALU.mult).then_inc(p.sev.sem, 1)))
            sch.run()
            sch = Sched(nc)
            ot_hist = [0, 0]
            for i in range(NT):
                bb = i % 2
                x_t = p.sxp[bb].inc(16)
                ep_ = p.sev.n
                sch.add("sync", lambda s, i=i, bb=bb, ep=ep_, ot=ot_hist[bb], x_t=x_t: (
                    s.wait_ge(p.sev.sem, ep),
                    s.wait_ge(p.sop[bb].sem, ot),
                    s.dma_start(rb2[bb][:], rtile(p.r_b, i)).then_inc(p.sxp[bb].sem, 16)))
                for ch in range(2):
                    b = (i * 2 + ch) % 2
                    fa = p.sev.n
                    m1 = p.smm.inc()
                    def dmf(t, i=i, ch=ch, b=b, fa=fa, w_t=w_t):
                        t.wait_ge(p.sev.sem, fa)
                        t.wait_ge(p.scc.sem, w_t)
                        for i2c in range(8):
                            ins = t.matmul(psA[b][:],
                                           PsS[:, i2c * T + i * 128: i2c * T + (i + 1) * 128],
                                           shd[:, i2c * 1024 + ch * 512: i2c * 1024 + ch * 512 + 512],
                                           start=(i2c == 0), stop=(i2c == 7))
                        ins.then_inc(p.smm.sem, 1)
                    sch.add("tensor", dmf)
                    p.sev.inc()
                    sch.add("vector", lambda v, ch=ch, b=b, bb=bb, m1=m1, x_t=x_t,
                            ot=ot_hist[bb]: (
                        v.wait_ge(p.smm.sem, m1),
                        v.wait_ge(p.sxp[bb].sem, x_t),
                        v.wait_ge(p.sop[bb].sem, ot),
                        v.tensor_tensor(rob[bb][:, ch * 512:(ch + 1) * 512], psA[b][:],
                                        rb2[bb][:, ch * 512:(ch + 1) * 512],
                                        op=ALU.add).then_inc(p.sev.sem, 1)))
                o_t = p.sop[bb].inc(16)
                lv = p.sev.n
                sch.add("sync", lambda s, i=i, bb=bb, lv=lv: (
                    s.wait_ge(p.sev.sem, lv),
                    s.dma_start(rtile(r_out, i), rob[bb][:]).then_inc(p.sop[bb].sem, 16)))
                ot_hist[bb] = o_t
            sch.add("sync", lambda s, n0=p.sop[0].n, n1=p.sop[1].n: (
                s.wait_ge(p.sop[0].sem, n0), s.wait_ge(p.sop[1].sem, n1)))
            sch.run()


def build_program(stages):
    nc = bass.Bass("TRN2", target_bir_lowering=False)
    with ExitStack() as ctx:
        p = Prog(nc, ctx)
        p.load_consts()
        if stages >= 4:
            queue_wc_cast(p)
        chain = [p.x, p.r_a, p.r_b, p.r_a]
        outs = [chain[i + 1] if i < stages - 1 else p.out for i in range(stages)]
        emit_l0_attn(p, p.x, outs[0])
        if stages >= 2:
            emit_l0_ffn(p, outs[0], outs[1])
        if stages >= 3:
            emit_l1_attn(p, outs[1], outs[2])
        if stages >= 4:
            emit_l1_ffn(p, outs[2], outs[3])
    return nc


def kernel(**inp):
    shared = _host_inputs(inp)
    x = inp["L0_attn_x"]
    nc = build_program(STAGES)
    in_maps = []
    for b in range(8):
        m = dict(shared)
        m["x"] = np.ascontiguousarray(x[b])
        in_maps.append(m)
    res = run_bass_kernel_spmd(nc, in_maps, core_ids=list(range(8)))
    if DEBUG:
        return res.results
    return np.stack([r["out"] for r in res.results], axis=0)
